# version 17
# baseline (speedup 1.0000x reference)
import contextlib
import numpy as np
import concourse.bass as bass
import concourse.mybir as mybir
from concourse.bass_utils import run_bass_kernel_spmd

F32 = mybir.dt.float32
BF16 = mybir.dt.bfloat16
AF = mybir.ActivationFunctionType
OP = mybir.AluOpType

D = 1024
KC = 8
TCORE = 4096
TT = 512
NT = 8
SUB = 4
I_FF = 4096
IC = 32
E = 4
G = 1
NG = IC // G
EPS1 = 1e-5
EPS2 = 1e-6
EPS3 = 1e-12
FORGET_LB = 0.9


def build_core(debug=False):
    nc = bass.Bass("TRN2", target_bir_lowering=False)

    xin = nc.dram_tensor("x", [TCORE, D], F32, kind="ExternalInput").ap()
    wf = nc.dram_tensor("wf", [D, D], F32, kind="ExternalInput").ap()
    wiw = nc.dram_tensor("wiw", [D, D], F32, kind="ExternalInput").ap()
    wow = nc.dram_tensor("wow", [D, D], F32, kind="ExternalInput").ap()
    norw = nc.dram_tensor("norw", [D], F32, kind="ExternalInput").ap()
    wr = nc.dram_tensor("wr", [D, E], F32, kind="ExternalInput").ap()
    w1 = nc.dram_tensor("w1", [E, D, I_FF], F32, kind="ExternalInput").ap()
    b1 = nc.dram_tensor("b1", [E, I_FF], F32, kind="ExternalInput").ap()
    wo = nc.dram_tensor("wo", [I_FF, D], F32, kind="ExternalInput").ap()
    bo = nc.dram_tensor("bo", [D], F32, kind="ExternalInput").ap()
    gam = nc.dram_tensor("gam", [D], F32, kind="ExternalInput").ap()
    bet = nc.dram_tensor("bet", [D], F32, kind="ExternalInput").ap()
    out = nc.dram_tensor("out", [TCORE, D], F32, kind="ExternalOutput").ap()

    sk = "ExternalOutput" if debug else "Internal"
    wnat_s = nc.dram_tensor("wnat_s", [3 * D, D], BF16, kind="Internal").ap()
    w1s = nc.dram_tensor("w1s", [128, 131072], BF16, kind="Internal").ap()
    wos = nc.dram_tensor("wos", [128, 32768], BF16, kind="Internal").ap()
    xnbf_s = nc.dram_tensor("xnbf_s", [NT * 128, KC * TT], BF16, kind=sk).ap()
    ss_s = nc.dram_tensor("ss_s", [NT * 128, KC * TT], BF16, kind=sk).ap()

    es = contextlib.ExitStack()
    with es:
        AS = nc.alloc_sbuf_tensor
        ident_bf = AS("ident_bf", [128, 128], BF16)
        ident_f = AS("ident_f", [128, 128], F32)
        ones_bf = AS("ones_bf", [128, 128], BF16)
        pidx = AS("pidx", [128, 1], F32)
        sel_bf = AS("sel_bf", [4, E * 128], BF16)
        wr_bf = AS("wr_bf", [128, KC * E], BF16)
        b1t_bf = AS("b1t_bf", [4, I_FF], BF16)
        norw_cm = AS("norw_cm", [128, KC], F32)
        gam_cm = AS("gam_cm", [128, KC], F32)
        bet_cm = AS("bet_cm", [128, KC], F32)
        bo_cm = AS("bo_cm", [128, KC], F32)
        stt = AS("stt", [128, 80], F32)
        mask_all = AS("mask_all", [128, NT * SUB * E], BF16)
        x_buf = [AS("x_a", [128, SUB * D], F32), AS("x_b", [128, SUB * D], F32)]

        sem_names = [
            "s_x0", "s_x1", "s_xfree", "gp_cst", "s_wn0", "s_wn1", "s_wn2", "s_wr", "s_bias", "s_wt", "s_wtB", "s_wt2", "s_prep", "s_sel",
            "ss_acc", "sv_st", "ss_rr", "sv_xn", "s_xnw0", "s_xnw1", "s_sw",
            "trA_R0", "trA_R1", "trA_F0", "trA_F1", "trB_R0", "trB_R1", "trB_F0", "trB_F1", "ss_xnev", "sv_x2ev",
            "pj_R0", "pj_F0", "pj_R1", "pj_F1",
            "ss_o", "vb_R0", "vb_F0", "vb_R1", "vb_F1",
            "sv_s", "bar",
            "s_xb0", "s_xb1", "s_sb", "sl_R0", "sl_R1", "sl_F", "wo_R0", "wo_R1", "wo_F",
            "s_lgR", "sv_lgF", "sv_mtm", "s_mtR", "sv_m4", "s_mbR", "ss_mb", "sv_msk",
            "pre_R0", "pre_F0", "pre_R1", "pre_F1", "s_predone", "ss_int",
            "y2_R0", "y2_F0", "y2_R1", "y2_F1", "s_wodone", "ss_y3",
            "sv_y3bf", "mu_R", "mu_F", "sq_R", "sq_F", "sv_var", "ss_sqrt",
            "sv_z", "ss_g", "sv_p", "s_pdone", "sv_out", "st_out",
        ]
        SM = {n: es.enter_context(nc.semaphore(n)) for n in sem_names}

        def ti(instr, name, k=1):
            instr.then_inc(SM[name], k)
            return instr

        def W(eng, name, val):
            if val > 0:
                eng.wait_ge(SM[name], val)

        def DR(eng, *names):
            ins = eng.drain()
            if names:
                ins.then_inc(SM[names[0]], 1)
                for n in names[1:]:
                    eng.nop().then_inc(SM[n], 1)

        with contextlib.ExitStack() as esA:
            EA = esA.enter_context
            iota_f = EA(nc.sbuf_tensor("iota_f", [128, 128], F32))
            sel_f = EA(nc.sbuf_tensor("sel_f", [4, E * 128], F32))
            junk = EA(nc.sbuf_tensor("junk", [128, 1024], F32))
            carry = EA(nc.sbuf_tensor("carry", [128, KC], F32))
            wfT = EA(nc.sbuf_tensor("wfT", [128, KC * D], BF16))
            wiT = EA(nc.sbuf_tensor("wiT", [128, KC * D], BF16))
            woT = EA(nc.sbuf_tensor("woT", [128, KC * D], BF16))
            xn_tm = EA(nc.sbuf_tensor("xn_tm", [128, SUB * D], BF16))
            xn2_tm = EA(nc.sbuf_tensor("xn2_tm", [128, SUB * D], BF16))
            xn_fm = [EA(nc.sbuf_tensor("xn_fm0", [128, KC * TT], BF16)),
                     EA(nc.sbuf_tensor("xn_fm1", [128, KC * TT], BF16))]
            xn2_fm = EA(nc.sbuf_tensor("xn2_fm", [128, KC * TT], BF16))
            f_fm = EA(nc.sbuf_tensor("f_fm", [128, KC * TT], F32))
            o_fm = EA(nc.sbuf_tensor("o_fm", [128, KC * TT], F32))
            negu_bf = EA(nc.sbuf_tensor("negu_bf", [128, KC * TT], BF16))
            negS_bf = EA(nc.sbuf_tensor("negS_bf", [128, KC * TT], BF16))
            v_bf = EA(nc.sbuf_tensor("v_bf", [128, KC * TT], BF16))
            s_bf = EA(nc.sbuf_tensor("s_bf", [128, KC * TT], BF16))
            trA = [EA(nc.psum_tensor("trA0", [128, TT], BF16)),
                   EA(nc.psum_tensor("trA1", [128, TT], BF16))]
            trB = [EA(nc.psum_tensor("trB0", [128, TT], BF16)),
                   EA(nc.psum_tensor("trB1", [128, TT], BF16))]
            pj = [EA(nc.psum_tensor("pj0", [128, TT], F32)),
                  EA(nc.psum_tensor("pj1", [128, TT], F32))]
            vbp = [EA(nc.psum_tensor("vb0", [128, TT], F32)),
                   EA(nc.psum_tensor("vb1", [128, TT], F32))]
            prt = trA[0].bitcast(F32)
            wTs = [wfT, wiT, woT]

            with nc.Block() as blockA:

                @blockA.gpsimd
                def _(g):
                    g.memset(ones_bf[:, :], 1.0)
                    g.iota(iota_f[:, :], [[1, 128]], channel_multiplier=0,
                           allow_small_or_imprecise_dtypes=True)
                    g.iota(sel_f[:, :].rearrange("p (e j) -> p e j", e=E), [[1, E], [0, 128]],
                           channel_multiplier=0, allow_small_or_imprecise_dtypes=True)
                    g.iota(pidx[:, :], [[1, 1]], channel_multiplier=1,
                           allow_small_or_imprecise_dtypes=True)
                    DR(g, "s_prep")
                    ti(g.dma_start(out=wnat_s[1 * D:2 * D, :], in_=wiw[:, :]), "s_wn0", 16)
                    ti(g.dma_start(out=wnat_s[0 * D:1 * D, :], in_=wf[:, :]), "s_wn1", 16)
                    ti(g.dma_start(out=wnat_s[2 * D:3 * D, :], in_=wow[:, :]), "s_wn2", 16)
                    wr_src = wr[:, :].rearrange("(k p) e -> p k e", p=128)
                    ti(g.dma_start(out=wr_bf[:, :].rearrange("p (k e) -> p k e", k=KC),
                                   in_=wr_src), "s_wr", 16)
                    ti(g.dma_start(out=b1t_bf[:, :], in_=b1[:, :]), "gp_cst", 16)
                    for h in range(KC):
                        src = wo[:, h * 128:(h + 1) * 128].rearrange("(ic p) hh -> p ic hh", p=128)
                        dst = wos[:, h * I_FF:(h + 1) * I_FF].rearrange("p (ic hh) -> p ic hh", ic=IC)
                        ti(g.dma_start(out=dst, in_=src), "gp_cst", 16)
                    for grp in range(NG):
                        for k in range(KC):
                            src = w1[:, k * 128:(k + 1) * 128,
                                     grp * G * 128:(grp + 1) * G * 128].rearrange("e p i -> p e i")
                            off = (grp * KC + k) * E * G * 128
                            dst = w1s[:, off:off + E * G * 128].rearrange("p (e i) -> p e i", e=E)
                            ti(g.dma_start(out=dst, in_=src), "gp_cst", 16)
                    W(g, "gp_cst", 4240)
                    g.nop().then_inc(SM["bar"], 1)

                @blockA.sync
                def _(sy):
                    def load_x(t):
                        W(sy, "s_xfree", t - 1)
                        src = xin[t * TT:(t + 1) * TT, :].rearrange("(s p) d -> p s d", p=128)
                        dst = x_buf[t % 2][:, :].rearrange("p (s d) -> p s d", s=SUB)
                        ti(sy.dma_start(out=dst, in_=src), "s_x%d" % (t % 2), 16)

                    load_x(0)
                    load_x(1)
                    for dst_, src_ in [(norw_cm, norw), (gam_cm, gam), (bet_cm, bet),
                                       (bo_cm, bo)]:
                        srcb = src_[:].rearrange("(k p) -> p k", p=128)
                        ti(sy.dma_start(out=dst_[:, :], in_=srcb,
                                        allow_slow_non_contiguous=True), "s_bias", 16)
                    W(sy, "s_wn0", 16)
                    for k in range(KC):
                        ti(sy.dma_start_transpose(
                            wiT[:, k * D:(k + 1) * D],
                            wnat_s[1 * D:2 * D, k * 128:(k + 1) * 128]), "s_wt", 16)
                    W(sy, "s_wn1", 16)
                    for k in range(KC):
                        ti(sy.dma_start_transpose(
                            wfT[:, k * D:(k + 1) * D],
                            wnat_s[0 * D:1 * D, k * 128:(k + 1) * 128]), "s_wtB", 16)
                    W(sy, "s_wn2", 16)
                    for k in range(KC):
                        ti(sy.dma_start_transpose(
                            woT[:, k * D:(k + 1) * D],
                            wnat_s[2 * D:3 * D, k * 128:(k + 1) * 128]), "s_wt2", 16)
                    for t in range(NT):
                        if t + 2 < NT:
                            load_x(t + 2)
                        W(sy, "ss_xnev", t + 1)
                        ti(sy.dma_start(out=xnbf_s[t * 128:(t + 1) * 128, :],
                                        in_=xn_fm[t % 2][:, :]), "s_xnw%d" % (t % 2), 16)
                        W(sy, "sv_s", t + 1)
                        ti(sy.dma_start(out=ss_s[t * 128:(t + 1) * 128, :], in_=s_bf[:, :]),
                           "s_sw", 16)
                    W(sy, "s_xnw0", 16 * NT // 2)
                    W(sy, "s_xnw1", 16 * NT // 2)
                    W(sy, "s_sw", 16 * NT)
                    sy.nop().then_inc(SM["bar"], 1)

                @blockA.scalar
                def _(sc):

                    def accum_piece(t, i):
                        if i == 0:
                            W(sc, "s_x%d" % (t % 2), 16 * (t // 2 + 1))
                            W(sc, "sv_st", t)
                        sc.drain()
                        sub = i % SUB
                        fn = AF.Copy if i < SUB else AF.Square
                        sc.activation(junk[:, :], x_buf[t % 2][:, sub * D:(sub + 1) * D],
                                      fn, accum_out=stt[:, i:i + 1])
                        if i == 7:
                            DR(sc, "ss_acc")

                    def accums(t):
                        for i in range(8):
                            accum_piece(t, i)

                    def sqrtf(t):
                        W(sc, "sv_st", t + 1)
                        ti(sc.activation(stt[:, 40:48], stt[:, 32:40], AF.Sqrt), "ss_rr", 1)

                    accums(0)
                    sqrtf(0)
                    accums(1)
                    for t in range(NT):
                        if t + 1 < NT:
                            sqrtf(t + 1)
                        for c in range(KC):
                            r = 24 * t + 8 + c
                            b = r % 2
                            W(sc, "pj_R%d" % b, r // 2 + 1)
                            ti(sc.activation(f_fm[:, c * TT:(c + 1) * TT], pj[b][:, :],
                                             AF.Sigmoid), "pj_F%d" % b, 1)
                            if c % 2 == 1 and t + 2 < NT:
                                accum_piece(t + 2, c // 2)
                        for c in range(KC):
                            r = 24 * t + 16 + c
                            b = r % 2
                            W(sc, "pj_R%d" % b, r // 2 + 1)
                            ti(sc.activation(o_fm[:, c * TT:(c + 1) * TT], pj[b][:, :],
                                             AF.Sigmoid), "pj_F%d" % b, 1)
                            if c % 2 == 1 and t + 2 < NT:
                                accum_piece(t + 2, 4 + c // 2)
                        DR(sc, "ss_o")
                    DR(sc, "bar")

                @blockA.vector
                def _(v):
                    W(v, "s_prep", 1)
                    W(v, "s_bias", 64)
                    v.tensor_scalar(ident_bf[:, :], iota_f[:, :], pidx[:, :], None, OP.is_equal)
                    v.tensor_scalar(ident_f[:, :], iota_f[:, :], pidx[:, :], None, OP.is_equal)
                    v.tensor_scalar(sel_bf[:, :], sel_f[:, :], pidx[0:4, :], None, OP.is_equal)
                    DR(v, "s_sel")

                    def vstats(t):
                        W(v, "ss_acc", t + 1)
                        v.tensor_scalar(stt[:, 8:12], stt[:, 0:4], 1.0 / D, None, OP.mult)
                        v.tensor_scalar(stt[:, 12:16], stt[:, 4:8], 1.0 / D, None, OP.mult)
                        v.drain()
                        v.tensor_tensor(stt[:, 16:20], stt[:, 8:12], stt[:, 8:12], OP.mult)
                        v.drain()
                        v.tensor_tensor(stt[:, 20:24], stt[:, 12:16], stt[:, 16:20], OP.subtract)
                        v.drain()
                        v.tensor_scalar(stt[:, 24:28], stt[:, 20:24], 1.0, EPS1, OP.mult, OP.add)
                        v.tensor_scalar(stt[:, 28:32], stt[:, 20:24], 1.0 + EPS2, EPS1 * EPS2,
                                        OP.mult, OP.add)
                        v.drain()
                        v.reciprocal(stt[:, 32:40], stt[:, 24:32])
                        DR(v, "sv_st")

                    def xnts(t):
                        W(v, "ss_rr", t + 1)
                        W(v, "trA_R0", 4 * t)
                        W(v, "trA_R1", 4 * t)
                        W(v, "trB_R0", 4 * t)
                        W(v, "trB_R1", 4 * t)
                        for sub in range(SUB):
                            v.tensor_scalar(xn_tm[:, sub * D:(sub + 1) * D],
                                            x_buf[t % 2][:, sub * D:(sub + 1) * D],
                                            stt[:, 8 + sub:9 + sub], stt[:, 40 + sub:41 + sub],
                                            OP.subtract, OP.mult)
                        for sub in range(SUB):
                            v.tensor_scalar(xn2_tm[:, sub * D:(sub + 1) * D],
                                            x_buf[t % 2][:, sub * D:(sub + 1) * D],
                                            stt[:, 8 + sub:9 + sub], stt[:, 44 + sub:45 + sub],
                                            OP.subtract, OP.mult)
                        DR(v, "sv_xn", "s_xfree")

                    vstats(0)
                    xnts(0)
                    vstats(1)
                    for t in range(NT):
                        W(v, "s_xnw%d" % (t % 2), 16 * (t // 2))
                        W(v, "sv_s", t - 1)
                        for k in range(KC):
                            g_ = 8 * t + k
                            p_ = g_ % 2
                            W(v, "trA_R%d" % p_, g_ // 2 + 1)
                            ti(v.tensor_copy(xn_fm[t % 2][:, k * TT:(k + 1) * TT],
                                             trA[p_][:, :]), "trA_F%d" % p_, 1)
                            W(v, "trB_R%d" % p_, g_ // 2 + 1)
                            ti(v.tensor_scalar(xn2_fm[:, k * TT:(k + 1) * TT], trB[p_][:, :],
                                               norw_cm[:, k:k + 1], None, OP.mult),
                               "trB_F%d" % p_, 1)
                        DR(v, "ss_xnev", "sv_x2ev")
                        for c in range(KC):
                            r = 24 * t + c
                            b = r % 2
                            W(v, "pj_R%d" % b, r // 2 + 1)
                            ti(v.tensor_copy(v_bf[:, c * TT:(c + 1) * TT], pj[b][:, :]),
                               "pj_F%d" % b, 1)
                        if t + 1 < NT:
                            xnts(t + 1)
                        W(v, "s_lgR", t + 1)
                        lg3 = prt[:, 0:SUB * E].rearrange("p (s e) -> p s e", e=E)
                        mk3 = mask_all[:, t * SUB * E:(t + 1) * SUB * E].rearrange(
                            "p (s e) -> p s e", e=E)
                        st3 = stt[:, :].rearrange("p (c o) -> p c o", o=1)
                        for sub in range(SUB):
                            v.tensor_reduce(stt[:, 48 + sub:49 + sub], lg3[:, sub:sub + 1, :],
                                            mybir.AxisListType.X, OP.max)
                        v.drain()
                        rmax = st3[:, 48:52, :]
                        v.tensor_tensor(st3[:, 52:56, :], lg3[:, :, 0:1], rmax, OP.is_ge)
                        v.tensor_tensor(st3[:, 60:64, :], lg3[:, :, 1:2], rmax, OP.is_ge)
                        ins = v.tensor_tensor(st3[:, 64:68, :], lg3[:, :, 2:3], rmax, OP.is_ge)
                        ti(ins, "sv_lgF", 1)
                        v.drain()
                        v.tensor_scalar(st3[:, 56:60, :], st3[:, 52:56, :], -1.0, 1.0,
                                        OP.mult, OP.add)
                        v.tensor_copy(mk3[:, :, 0:1], st3[:, 52:56, :])
                        v.drain()
                        v.tensor_tensor(st3[:, 60:64, :], st3[:, 60:64, :], st3[:, 56:60, :],
                                        OP.mult)
                        v.drain()
                        v.tensor_tensor(st3[:, 56:60, :], st3[:, 56:60, :], st3[:, 60:64, :],
                                        OP.subtract)
                        v.tensor_copy(mk3[:, :, 1:2], st3[:, 60:64, :])
                        v.drain()
                        v.tensor_tensor(st3[:, 64:68, :], st3[:, 64:68, :], st3[:, 56:60, :],
                                        OP.mult)
                        v.drain()
                        v.tensor_copy(mk3[:, :, 2:3], st3[:, 64:68, :])
                        v.tensor_tensor(st3[:, 56:60, :], st3[:, 56:60, :], st3[:, 64:68, :],
                                        OP.subtract)
                        v.drain()
                        ti(v.tensor_copy(mk3[:, :, 3:4], st3[:, 56:60, :]), "sv_mtm", 1)
                        v.drain()
                        for c in range(KC):
                            r = 24 * t + 8 + c
                            b = r % 2
                            W(v, "pj_F%d" % b, r // 2 + 1)
                            v.tensor_scalar(f_fm[:, c * TT:(c + 1) * TT],
                                            f_fm[:, c * TT:(c + 1) * TT], FORGET_LB, None,
                                            OP.max)
                        v.drain()
                        for h in range(KC):
                            b = h % 2
                            W(v, "vb_R%d" % b, 4 * t + h // 2 + 1)
                            ti(v.scalar_tensor_tensor(negu_bf[:, h * TT:(h + 1) * TT],
                                                      f_fm[:, h * TT:(h + 1) * TT], 1.0,
                                                      vbp[b][:, :], OP.subtract, OP.mult),
                               "vb_F%d" % b, 1)
                        v.drain()
                        if t % 4 != 0:
                            nS3 = negS_bf[:, :].rearrange("p (h tt) -> p h tt", h=KC)
                            v.tensor_copy(carry[:, :].rearrange("p (h o) -> p h o", o=1),
                                          nS3[:, :, TT - 1:TT])
                            v.drain()
                        for h in range(KC):
                            init = 0.0 if t % 4 == 0 else carry[:, h:h + 1]
                            v.tensor_tensor_scan(negS_bf[:, h * TT:(h + 1) * TT],
                                                 f_fm[:, h * TT:(h + 1) * TT],
                                                 negu_bf[:, h * TT:(h + 1) * TT],
                                                 init, OP.mult, OP.add)
                        if t + 2 < NT:
                            vstats(t + 2)
                        W(v, "ss_o", t + 1)
                        v.drain()
                        v.tensor_tensor(negu_bf[:, :], o_fm[:, :], negS_bf[:, :], OP.mult)
                        W(v, "s_sw", 16 * t)
                        W(v, "ss_xnev", t + 1)
                        v.drain()
                        ti(v.tensor_tensor(s_bf[:, :], xn_fm[t % 2][:, :], negu_bf[:, :],
                                           OP.subtract), "sv_s", 1)
                    DR(v, "bar")

                @blockA.tensor
                def _(te):
                    W(te, "s_sel", 1)
                    for t in range(NT):
                        W(te, "sv_xn", t + 1)
                        W(te, "sv_lgF", t)
                        for k in range(KC):
                            g_ = 8 * t + k
                            p_ = g_ % 2
                            W(te, "trA_F%d" % p_, g_ // 2)
                            ins = None
                            for sub in range(SUB):
                                ins = te.transpose(
                                    trA[p_][:, sub * 128:(sub + 1) * 128],
                                    xn_tm[:, sub * D + k * 128: sub * D + (k + 1) * 128],
                                    ident_bf[:, :])
                            ti(ins, "trA_R%d" % p_, 1)
                            W(te, "trB_F%d" % p_, g_ // 2)
                            for sub in range(SUB):
                                ins = te.transpose(
                                    trB[p_][:, sub * 128:(sub + 1) * 128],
                                    xn2_tm[:, sub * D + k * 128: sub * D + (k + 1) * 128],
                                    ident_bf[:, :])
                            ti(ins, "trB_R%d" % p_, 1)
                        W(te, "ss_xnev", t + 1)
                        if t == 0:
                            W(te, "s_wr", 16)
                        ins = None
                        for sub in range(SUB):
                            for k in range(KC):
                                ins = te.matmul(
                                    prt[:, sub * E:(sub + 1) * E],
                                    xn_fm[t % 2][:, k * TT + sub * 128: k * TT + (sub + 1) * 128],
                                    wr_bf[:, k * E:(k + 1) * E],
                                    start=(k == 0), stop=(k == KC - 1))
                        ti(ins, "s_lgR", 1)
                        W(te, "sv_x2ev", t + 1)
                        for pi, wT in enumerate([wiT, wfT, woT]):
                            if t == 0:
                                if pi == 0:
                                    W(te, "s_wt", 16 * 8)
                                elif pi == 1:
                                    W(te, "s_wtB", 16 * 8)
                                else:
                                    W(te, "s_wt2", 16 * 8)
                            for c in range(KC):
                                r = 24 * t + pi * KC + c
                                b = r % 2
                                W(te, "pj_F%d" % b, r // 2)
                                ins = None
                                for k in range(KC):
                                    ins = te.matmul(pj[b][:, :],
                                                    wT[:, k * D + c * 128: k * D + (c + 1) * 128],
                                                    xn2_fm[:, k * TT:(k + 1) * TT],
                                                    start=(k == 0), stop=(k == KC - 1))
                                ti(ins, "pj_R%d" % b, 1)
                            if pi == 1:
                                for h in range(KC):
                                    b = h % 2
                                    r = 24 * t + h
                                    W(te, "pj_F%d" % (r % 2), r // 2 + 1)
                                    W(te, "vb_F%d" % b, 4 * t + h // 2)
                                    ti(te.matmul(vbp[b][:, :], ones_bf[:, :],
                                                 v_bf[:, h * TT:(h + 1) * TT],
                                                 start=True, stop=True), "vb_R%d" % b, 1)
                    DR(te, "bar")

        with contextlib.ExitStack() as esB:
            EB = esB.enter_context
            w1sl = [EB(nc.sbuf_tensor("w1sl0", [128, KC * E * G * 128], BF16)),
                    EB(nc.sbuf_tensor("w1sl1", [128, KC * E * G * 128], BF16))]
            wosl = [EB(nc.sbuf_tensor("wosl0", [128, IC * 128], BF16)),
                    EB(nc.sbuf_tensor("wosl1", [128, IC * 128], BF16))]
            maskedA = EB(nc.sbuf_tensor("maskedA", [128, E * KC * TT], BF16))
            interA = EB(nc.sbuf_tensor("interA", [128, IC * TT], BF16))
            xn_b = [EB(nc.sbuf_tensor("xn_b0", [128, KC * TT], BF16)),
                    EB(nc.sbuf_tensor("xn_b1", [128, KC * TT], BF16))]
            s_b = EB(nc.sbuf_tensor("s_b", [128, KC * TT], BF16))
            y3 = EB(nc.sbuf_tensor("y3", [128, KC * TT], F32))
            g_bf = EB(nc.sbuf_tensor("g_bf", [128, KC * TT], BF16))
            mask4_bf = EB(nc.sbuf_tensor("mask4_bf", [4, TT], BF16))
            mu_rep = EB(nc.sbuf_tensor("mu_rep", [128, TT], F32))
            rstd_rep = EB(nc.sbuf_tensor("rstd_rep", [128, TT], F32))
            mb_bf = EB(nc.sbuf_tensor("mb_bf", [128, E * TT], BF16))
            jrow = EB(nc.sbuf_tensor("jrow", [128, TT], F32))
            p_bf = EB(nc.sbuf_tensor("p_bf", [128, KC * TT], BF16))


            out_tm = y3

            pre = [EB(nc.psum_tensor("pre0", [128, TT], F32)),
                   EB(nc.psum_tensor("pre1", [128, TT], F32))]
            y2 = [EB(nc.psum_tensor("y2a", [128, TT], F32)),
                  EB(nc.psum_tensor("y2b", [128, TT], F32))]
            pmu = EB(nc.psum_tensor("pmu", [128, TT], F32))
            psq = EB(nc.psum_tensor("psq", [128, TT], F32))
            pmb = EB(nc.psum_tensor("pmb", [128, TT], F32))
            pmb2 = EB(nc.psum_tensor("pmb2", [128, TT], F32))
            pmb_bf = pmb.bitcast(BF16)
            pmb2_bf = pmb2.bitcast(BF16)
            pmu_bf = pmu.bitcast(BF16)
            psq_bf = psq.bitcast(BF16)

            y3bf = p_bf
            y3sq = g_bf

            def gsl(k0, k1):
                return g_bf[:, k0 * TT:k1 * TT]

            with nc.Block() as blockB:

                @blockB.sync
                def _(sy):
                    W(sy, "bar", 5)
                    ti(sy.dma_start(out=xn_b[0][:, :], in_=xnbf_s[0:128, :]), "s_xb0", 16)
                    for t in range(NT):
                        W(sy, "s_xfree", t + 7)
                        src = xin[t * TT:(t + 1) * TT, :].rearrange("(s p) d -> p s d", p=128)
                        dst = x_buf[t % 2][:, :].rearrange("p (s d) -> p s d", s=SUB)
                        ti(sy.dma_start(out=dst, in_=src), "s_x%d" % (t % 2), 16)
                        if t + 1 < NT:
                            W(sy, "y2_R1", 4 * t)
                            W(sy, "sv_msk", t)
                            ti(sy.dma_start(out=xn_b[(t + 1) % 2][:, :],
                                            in_=xnbf_s[(t + 1) * 128:(t + 2) * 128, :]),
                               "s_xb%d" % ((t + 1) % 2), 16)
                        for grp in range(NG):
                            gi = t * NG + grp
                            if gi == 0:
                                W(sy, "gp_cst", 4240)
                            W(sy, "sl_F", gi - 1)
                            ti(sy.dma_start(out=w1sl[gi % 2][:, :],
                                            in_=w1s[:, grp * 4096:(grp + 1) * 4096]),
                               "sl_R%d" % (gi % 2), 16)
                        for h in range(KC):
                            wi_ = t * KC + h
                            W(sy, "wo_F", wi_ - 1)
                            ti(sy.dma_start(out=wosl[wi_ % 2][:, :],
                                            in_=wos[:, h * I_FF:(h + 1) * I_FF]),
                               "wo_R%d" % (wi_ % 2), 16)
                        W(sy, "sv_p", t)
                        ti(sy.dma_start(out=s_b[:, :], in_=ss_s[t * 128:(t + 1) * 128, :]),
                           "s_sb", 16)
                    sy.wait_ge(SM["st_out"], 16 * NT)

                @blockB.gpsimd
                def _(g):
                    W(g, "bar", 5)
                    for t in range(NT):
                        W(g, "sv_out", t + 1)
                        W(g, "st_out", 16 * t)
                        dsto = out[t * TT:(t + 1) * TT, :].rearrange("(s p) d -> p s d", p=128)
                        srco = out_tm[:, :].rearrange("p (s d) -> p s d", s=SUB)
                        ti(g.dma_start(out=dsto, in_=srco), "st_out", 16)
                    g.wait_ge(SM["st_out"], 16 * NT)

                @blockB.tensor
                def _(te):
                    W(te, "bar", 5)

                    def emit_ptr(tau):
                        W(te, "sv_p", tau + 1)
                        for sub in range(SUB):
                            W(te, "mu_F", 5 * tau + sub + 1)
                            ins = None
                            for kk in range(4):
                                ins = te.transpose(
                                    pmu_bf[:, kk * 128:(kk + 1) * 128],
                                    p_bf[:, kk * TT + sub * 128: kk * TT + (sub + 1) * 128],
                                    ident_bf[:, :])
                            ti(ins, "mu_R", 1)
                            W(te, "sq_F", 5 * tau + sub + 1)
                            for kk in range(4):
                                ins = te.transpose(
                                    psq_bf[:, kk * 128:(kk + 1) * 128],
                                    p_bf[:, (4 + kk) * TT + sub * 128: (4 + kk) * TT + (sub + 1) * 128],
                                    ident_bf[:, :])
                            ti(ins, "sq_R", 1)

                    def mask_prep(tau):
                        W(te, "ss_mb", 2 * tau)
                        ins = None
                        for sub in range(SUB):
                            ins = te.transpose(
                                pmb2_bf[0:4, sub * 128:(sub + 1) * 128],
                                mask_all[:, (tau * SUB + sub) * E:(tau * SUB + sub + 1) * E],
                                ident_bf[:, :])
                        ti(ins, "s_mtR", 1)
                        W(te, "sv_m4", tau + 1)
                        te.matmul(pmb[:, :], sel_bf[:, 0:128], mask4_bf[:, :],
                                  start=True, stop=True)
                        ti(te.matmul(pmb2[:, :], sel_bf[:, 128:256], mask4_bf[:, :],
                                     start=True, stop=True), "s_mbR", 1)
                        W(te, "ss_mb", 2 * tau + 1)
                        te.matmul(pmb[:, :], sel_bf[:, 256:384], mask4_bf[:, :],
                                  start=True, stop=True)
                        ti(te.matmul(pmb2[:, :], sel_bf[:, 384:512], mask4_bf[:, :],
                                     start=True, stop=True), "s_mbR", 1)

                    mask_prep(0)
                    for t in range(NT):
                        W(te, "sv_msk", t + 1)
                        for grp in range(NG):
                            if grp == NG // 2 and t > 0:
                                emit_ptr(t - 1)
                            gi = t * NG + grp
                            W(te, "sl_R%d" % (gi % 2), 16 * (gi // 2 + 1))
                            for il in range(G):
                                i_abs = grp * G + il
                                b = i_abs % 2
                                pr = t * IC + i_abs
                                W(te, "pre_F%d" % b, pr // 2)
                                first = True
                                for e in range(E):
                                    for k in range(KC):
                                        off = ((k * E + e) * G + il) * 128
                                        ins = te.matmul(
                                            pre[b][:, :], w1sl[gi % 2][:, off:off + 128],
                                            maskedA[:, (e * KC + k) * TT:(e * KC + k + 1) * TT],
                                            start=first, stop=False)
                                        first = False
                                        if il == G - 1 and e == E - 1 and k == KC - 1:
                                            ti(ins, "sl_F", 1)
                                ti(te.matmul(pre[b][:, :],
                                             b1t_bf[:, i_abs * 128:(i_abs + 1) * 128],
                                             mask4_bf[:, :], start=False, stop=True),
                                   "pre_R%d" % b, 1)
                        if t + 1 < NT:
                            mask_prep(t + 1)
                        W(te, "ss_int", t + 1)
                        for h in range(KC):
                            wi_ = t * KC + h
                            W(te, "wo_R%d" % (wi_ % 2), 16 * (wi_ // 2 + 1))
                            b = h % 2
                            W(te, "y2_F%d" % b, wi_ // 2)
                            ins = None
                            for ic in range(IC):
                                ins = te.matmul(y2[b][:, :],
                                                wosl[wi_ % 2][:, ic * 128:(ic + 1) * 128],
                                                interA[:, ic * TT:(ic + 1) * TT],
                                                start=(ic == 0), stop=False)
                            ti(ins, "wo_F", 1)
                            ti(te.matmul(y2[b][:, :], ident_bf[:, :],
                                         xn_b[t % 2][:, h * TT:(h + 1) * TT],
                                         start=False, stop=True), "y2_R%d" % b, 1)
                        W(te, "sv_y3bf", t + 1)
                        W(te, "mu_F", 5 * t)
                        W(te, "sq_F", 5 * t)
                        ins = None
                        for k in range(KC):
                            ins = te.matmul(pmu[:, :], ones_bf[:, :],
                                            y3bf[:, k * TT:(k + 1) * TT],
                                            start=(k == 0), stop=(k == KC - 1))
                        ti(ins, "mu_R", 1)
                        for k in range(KC):
                            ins = te.matmul(psq[:, :], ones_bf[:, :],
                                            y3sq[:, k * TT:(k + 1) * TT],
                                            start=(k == 0), stop=(k == KC - 1))
                        ti(ins, "sq_R", 1)
                    emit_ptr(NT - 1)

                @blockB.vector
                def _(v):
                    W(v, "bar", 5)

                    def final_adds(tau):
                        W(v, "s_x%d" % (tau % 2), 16 * (NT // 2 + tau // 2 + 1))
                        for sub in range(SUB):
                            W(v, "mu_R", 5 * tau + 2 + sub)
                            ti(v.tensor_tensor(out_tm[:, sub * D: sub * D + TT],
                                               x_buf[tau % 2][:, sub * D: sub * D + TT],
                                               pmu_bf[:, 0:TT], OP.add), "mu_F", 1)
                            W(v, "sq_R", 5 * tau + 2 + sub)
                            ti(v.tensor_tensor(out_tm[:, sub * D + TT: (sub + 1) * D],
                                               x_buf[tau % 2][:, sub * D + TT: (sub + 1) * D],
                                               psq_bf[:, 0:TT], OP.add), "sq_F", 1)
                        DR(v, "sv_out", "s_xfree")

                    def tail(tau):
                        W(v, "mu_R", 5 * tau + 1)
                        W(v, "sq_R", 5 * tau + 1)
                        ti(v.tensor_scalar(mu_rep[:, :], pmu[:, :], 1.0 / D, None, OP.mult),
                           "mu_F", 1)
                        ti(v.tensor_scalar(rstd_rep[:, :], psq[:, :], 1.0 / D, None, OP.mult),
                           "sq_F", 1)
                        v.drain()
                        v.tensor_tensor(jrow[:, :], mu_rep[:, :], mu_rep[:, :], OP.mult)
                        v.drain()
                        v.scalar_tensor_tensor(rstd_rep[:, :], rstd_rep[:, :], EPS3,
                                               jrow[:, :], OP.add, OP.subtract)
                        v.drain()
                        v.reciprocal(jrow[:, :], rstd_rep[:, :])
                        DR(v, "sv_var")
                        W(v, "ss_sqrt", tau + 1)
                        for k in range(KC):
                            v.tensor_tensor(y3[:, k * TT:(k + 1) * TT],
                                            y3[:, k * TT:(k + 1) * TT], mu_rep[:, :],
                                            OP.subtract)
                        v.drain()
                        for k in range(KC):
                            v.tensor_tensor(y3[:, k * TT:(k + 1) * TT],
                                            y3[:, k * TT:(k + 1) * TT], rstd_rep[:, :],
                                            OP.mult)
                        DR(v, "sv_z")
                        W(v, "ss_g", 8 * (tau + 1))
                        W(v, "s_sb", 16 * (tau + 1))
                        ti(v.tensor_tensor(p_bf[:, :], s_b[:, :], gsl(0, KC), OP.mult),
                           "sv_p", 1)

                    def mask_block(tau):
                        W(v, "s_mtR", tau + 1)
                        W(v, "pre_R1", 16 * tau)
                        ti(v.tensor_copy(mask4_bf[:, :], pmb2_bf[0:4, 0:TT]), "sv_m4", 1)
                        W(v, "ss_mb", 2 * (tau + 1))
                        W(v, "s_xb%d" % (tau % 2), 16 * (tau // 2 + 1))
                        ins = None
                        for e in range(E):
                            for k in range(KC):
                                ins = v.tensor_tensor(
                                    maskedA[:, (e * KC + k) * TT:(e * KC + k + 1) * TT],
                                    xn_b[tau % 2][:, k * TT:(k + 1) * TT],
                                    mb_bf[:, e * TT:(e + 1) * TT], OP.mult)
                        DR(v, "sv_msk")

                    mask_block(0)
                    for t in range(NT):
                        if t > 0:
                            tail(t - 1)
                            final_adds(t - 1)
                        if t + 1 < NT:
                            mask_block(t + 1)
                        W(v, "mu_R", 5 * t)
                        W(v, "sq_R", 5 * t)
                        for h in range(KC):
                            W(v, "y2_F%d" % (h % 2), (t * KC + h) // 2 + 1)
                            v.tensor_copy(y3bf[:, h * TT:(h + 1) * TT],
                                          y3[:, h * TT:(h + 1) * TT])
                            v.tensor_tensor(y3sq[:, h * TT:(h + 1) * TT],
                                            y3[:, h * TT:(h + 1) * TT],
                                            y3[:, h * TT:(h + 1) * TT], OP.mult)
                        DR(v, "sv_y3bf")
                    tail(NT - 1)
                    final_adds(NT - 1)

                @blockB.scalar
                def _(sc):
                    W(sc, "bar", 5)

                    def sqrtp(tau):
                        W(sc, "sv_var", tau + 1)
                        ti(sc.activation(rstd_rep[:, :], jrow[:, :], AF.Sqrt), "ss_sqrt", 1)

                    def silup(tau):
                        W(sc, "sv_z", tau + 1)
                        for k in range(KC):
                            ti(sc.activation(gsl(k, k + 1), y3[:, k * TT:(k + 1) * TT],
                                             AF.Silu, bias=bet_cm[:, k:k + 1],
                                             scale=gam_cm[:, k:k + 1]), "ss_g", 1)

                    def mb_evacs(tau):
                        W(sc, "s_mbR", 2 * tau + 1)
                        sc.copy(mb_bf[:, 0:TT], pmb[:, :])
                        ti(sc.copy(mb_bf[:, TT:2 * TT], pmb2[:, :]), "ss_mb", 1)
                        W(sc, "s_mbR", 2 * tau + 2)
                        sc.copy(mb_bf[:, 2 * TT:3 * TT], pmb[:, :])
                        ti(sc.copy(mb_bf[:, 3 * TT:4 * TT], pmb2[:, :]), "ss_mb", 1)

                    mb_evacs(0)
                    for t in range(NT):
                        sc.drain()
                        W(sc, "y2_R1", 4 * t)
                        for i_abs in range(IC):
                            if i_abs == 4 and t > 0:
                                sqrtp(t - 1)
                            if i_abs == 12 and t > 0:
                                silup(t - 1)
                            b = i_abs % 2
                            pr = t * IC + i_abs
                            W(sc, "pre_R%d" % b, pr // 2 + 1)
                            ti(sc.activation(interA[:, i_abs * TT:(i_abs + 1) * TT],
                                             pre[b][:, :], AF.Gelu), "pre_F%d" % b, 1)
                        DR(sc, "ss_int")
                        if t + 1 < NT:
                            mb_evacs(t + 1)
                        W(sc, "st_out", 16 * t)
                        for h in range(KC):
                            wi_ = t * KC + h
                            b = h % 2
                            W(sc, "y2_R%d" % b, wi_ // 2 + 1)
                            ti(sc.activation(y3[:, h * TT:(h + 1) * TT], y2[b][:, :],
                                             AF.Identity, bias=bo_cm[:, h:h + 1]),
                               "y2_F%d" % b, 1)
                        DR(sc, "ss_y3")
                    sqrtp(NT - 1)
                    silup(NT - 1)


    return nc


_NC_CACHE = {}


def kernel(L0_attn_x, L0_attn_norm_w, L0_attn_w_forget, L0_attn_w_input,
           L0_attn_w_output, L0_ffn_W_router, L0_ffn_W1, L0_ffn_b1, L0_ffn_Wo,
           L0_ffn_bo, L0_ffn_ln_gamma, L0_ffn_ln_beta):
    x = np.ascontiguousarray(np.asarray(L0_attn_x, dtype=np.float32))
    weights = {
        "wf": np.ascontiguousarray(np.asarray(L0_attn_w_forget, np.float32)),
        "wiw": np.ascontiguousarray(np.asarray(L0_attn_w_input, np.float32)),
        "wow": np.ascontiguousarray(np.asarray(L0_attn_w_output, np.float32)),
        "norw": np.ascontiguousarray(np.asarray(L0_attn_norm_w, np.float32)),
        "wr": np.ascontiguousarray(np.asarray(L0_ffn_W_router, np.float32)),
        "w1": np.ascontiguousarray(np.asarray(L0_ffn_W1, np.float32)),
        "b1": np.ascontiguousarray(np.asarray(L0_ffn_b1, np.float32)),
        "wo": np.ascontiguousarray(np.asarray(L0_ffn_Wo, np.float32)),
        "bo": np.ascontiguousarray(np.asarray(L0_ffn_bo, np.float32)),
        "gam": np.ascontiguousarray(np.asarray(L0_ffn_ln_gamma, np.float32)),
        "bet": np.ascontiguousarray(np.asarray(L0_ffn_ln_beta, np.float32)),
    }
    if "nc" not in _NC_CACHE:
        _NC_CACHE["nc"] = build_core()
    nc = _NC_CACHE["nc"]
    in_maps = []
    for c in range(8):
        shard = x[2 * c:2 * c + 2].reshape(TCORE, D)
        m = {"x": np.ascontiguousarray(shard)}
        m.update(weights)
        in_maps.append(m)
    res = run_bass_kernel_spmd(nc, in_maps, core_ids=list(range(8)))
    outs = [np.asarray(r["out"]).reshape(2, 2048, D) for r in res.results]
    return np.concatenate(outs, axis=0).astype(np.float32)


# version 18
# speedup vs baseline: 1.0007x; 1.0007x over previous
import contextlib
import numpy as np
import concourse.bass as bass
import concourse.mybir as mybir
from concourse.bass_utils import run_bass_kernel_spmd

F32 = mybir.dt.float32
BF16 = mybir.dt.bfloat16
AF = mybir.ActivationFunctionType
OP = mybir.AluOpType

D = 1024
KC = 8
TCORE = 4096
TT = 512
NT = 8
SUB = 4
I_FF = 4096
IC = 32
E = 4
G = 1
NG = IC // G
EPS1 = 1e-5
EPS2 = 1e-6
EPS3 = 1e-12
FORGET_LB = 0.9


def build_core(debug=False):
    nc = bass.Bass("TRN2", target_bir_lowering=False)

    xin = nc.dram_tensor("x", [TCORE, D], F32, kind="ExternalInput").ap()
    wf = nc.dram_tensor("wf", [D, D], F32, kind="ExternalInput").ap()
    wiw = nc.dram_tensor("wiw", [D, D], F32, kind="ExternalInput").ap()
    wow = nc.dram_tensor("wow", [D, D], F32, kind="ExternalInput").ap()
    norw = nc.dram_tensor("norw", [D], F32, kind="ExternalInput").ap()
    wr = nc.dram_tensor("wr", [D, E], F32, kind="ExternalInput").ap()
    w1 = nc.dram_tensor("w1", [E, D, I_FF], F32, kind="ExternalInput").ap()
    b1 = nc.dram_tensor("b1", [E, I_FF], F32, kind="ExternalInput").ap()
    wo = nc.dram_tensor("wo", [I_FF, D], F32, kind="ExternalInput").ap()
    bo = nc.dram_tensor("bo", [D], F32, kind="ExternalInput").ap()
    gam = nc.dram_tensor("gam", [D], F32, kind="ExternalInput").ap()
    bet = nc.dram_tensor("bet", [D], F32, kind="ExternalInput").ap()
    out = nc.dram_tensor("out", [TCORE, D], F32, kind="ExternalOutput").ap()

    sk = "ExternalOutput" if debug else "Internal"
    wnat_s = nc.dram_tensor("wnat_s", [3 * D, D], BF16, kind="Internal").ap()
    w1s = nc.dram_tensor("w1s", [128, 131072], BF16, kind="Internal").ap()
    wos = nc.dram_tensor("wos", [128, 32768], BF16, kind="Internal").ap()
    xnbf_s = nc.dram_tensor("xnbf_s", [NT * 128, KC * TT], BF16, kind=sk).ap()
    ss_s = nc.dram_tensor("ss_s", [NT * 128, KC * TT], BF16, kind=sk).ap()

    es = contextlib.ExitStack()
    with es:
        AS = nc.alloc_sbuf_tensor
        ident_bf = AS("ident_bf", [128, 128], BF16)
        ident_f = AS("ident_f", [128, 128], F32)
        ones_bf = AS("ones_bf", [128, 128], BF16)
        pidx = AS("pidx", [128, 1], F32)
        sel_bf = AS("sel_bf", [4, E * 128], BF16)
        wr_bf = AS("wr_bf", [128, KC * E], BF16)
        b1t_bf = AS("b1t_bf", [4, I_FF], BF16)
        norw_cm = AS("norw_cm", [128, KC], F32)
        gam_cm = AS("gam_cm", [128, KC], F32)
        bet_cm = AS("bet_cm", [128, KC], F32)
        bo_cm = AS("bo_cm", [128, KC], F32)
        stt = AS("stt", [128, 80], F32)
        mask_all = AS("mask_all", [128, NT * SUB * E], BF16)
        x_buf = [AS("x_a", [128, SUB * D], F32), AS("x_b", [128, SUB * D], F32)]

        sem_names = [
            "s_x0", "s_x1", "s_xfree", "gp_cst", "s_wn0", "s_wn1", "s_wn2", "s_wr", "s_bias", "s_wt", "s_wtB", "s_wt2", "s_prep", "s_sel",
            "ss_acc", "sv_st", "ss_rr", "sv_xn", "s_xnw0", "s_xnw1", "s_sw",
            "trA_R0", "trA_R1", "trA_F0", "trA_F1", "trB_R0", "trB_R1", "trB_F0", "trB_F1", "ss_xnev", "sv_x2ev",
            "pj_R0", "pj_F0", "pj_R1", "pj_F1",
            "ss_o", "vb_R0", "vb_F0", "vb_R1", "vb_F1",
            "sv_s", "bar",
            "s_xb0", "s_xb1", "s_sb", "sl_R0", "sl_R1", "sl_F", "wo_R0", "wo_R1", "wo_F",
            "s_lgR", "sv_lgF", "sv_mtm", "s_mtR", "sv_m4", "s_mbR", "ss_mb", "sv_msk",
            "pre_R0", "pre_F0", "pre_R1", "pre_F1", "s_predone", "ss_int",
            "y2_R0", "y2_F0", "y2_R1", "y2_F1", "s_wodone", "ss_y3",
            "sv_y3bf", "mu_R", "mu_F", "sq_R", "sq_F", "sv_var", "ss_sqrt",
            "sv_z", "ss_g", "sv_p", "s_pdone", "sv_out", "st_out",
        ]
        SM = {n: es.enter_context(nc.semaphore(n)) for n in sem_names}

        def ti(instr, name, k=1):
            instr.then_inc(SM[name], k)
            return instr

        def W(eng, name, val):
            if val > 0:
                eng.wait_ge(SM[name], val)

        def DR(eng, *names):
            ins = eng.drain()
            if names:
                ins.then_inc(SM[names[0]], 1)
                for n in names[1:]:
                    eng.nop().then_inc(SM[n], 1)

        with contextlib.ExitStack() as esA:
            EA = esA.enter_context
            iota_f = EA(nc.sbuf_tensor("iota_f", [128, 128], F32))
            sel_f = EA(nc.sbuf_tensor("sel_f", [4, E * 128], F32))
            junk = EA(nc.sbuf_tensor("junk", [128, 1024], F32))
            carry = EA(nc.sbuf_tensor("carry", [128, KC], F32))
            wfT = EA(nc.sbuf_tensor("wfT", [128, KC * D], BF16))
            wiT = EA(nc.sbuf_tensor("wiT", [128, KC * D], BF16))
            woT = EA(nc.sbuf_tensor("woT", [128, KC * D], BF16))
            xn_tm = EA(nc.sbuf_tensor("xn_tm", [128, SUB * D], BF16))
            xn2_tm = EA(nc.sbuf_tensor("xn2_tm", [128, SUB * D], BF16))
            xn_fm = [EA(nc.sbuf_tensor("xn_fm0", [128, KC * TT], BF16)),
                     EA(nc.sbuf_tensor("xn_fm1", [128, KC * TT], BF16))]
            xn2_fm = EA(nc.sbuf_tensor("xn2_fm", [128, KC * TT], BF16))
            f_fm = EA(nc.sbuf_tensor("f_fm", [128, KC * TT], F32))
            o_fm = EA(nc.sbuf_tensor("o_fm", [128, KC * TT], F32))
            negu_bf = EA(nc.sbuf_tensor("negu_bf", [128, KC * TT], BF16))
            negS_bf = EA(nc.sbuf_tensor("negS_bf", [128, KC * TT], BF16))
            v_bf = EA(nc.sbuf_tensor("v_bf", [128, KC * TT], BF16))
            s_bf = EA(nc.sbuf_tensor("s_bf", [128, KC * TT], BF16))
            trA = [EA(nc.psum_tensor("trA0", [128, TT], BF16)),
                   EA(nc.psum_tensor("trA1", [128, TT], BF16))]
            trB = [EA(nc.psum_tensor("trB0", [128, TT], BF16)),
                   EA(nc.psum_tensor("trB1", [128, TT], BF16))]
            pj = [EA(nc.psum_tensor("pj0", [128, TT], F32)),
                  EA(nc.psum_tensor("pj1", [128, TT], F32))]
            vbp = [EA(nc.psum_tensor("vb0", [128, TT], F32)),
                   EA(nc.psum_tensor("vb1", [128, TT], F32))]
            prt = trA[0].bitcast(F32)
            wTs = [wfT, wiT, woT]

            with nc.Block() as blockA:

                @blockA.gpsimd
                def _(g):
                    g.memset(ones_bf[:, :], 1.0)
                    g.iota(iota_f[:, :], [[1, 128]], channel_multiplier=0,
                           allow_small_or_imprecise_dtypes=True)
                    g.iota(sel_f[:, :].rearrange("p (e j) -> p e j", e=E), [[1, E], [0, 128]],
                           channel_multiplier=0, allow_small_or_imprecise_dtypes=True)
                    g.iota(pidx[:, :], [[1, 1]], channel_multiplier=1,
                           allow_small_or_imprecise_dtypes=True)
                    DR(g, "s_prep")
                    ti(g.dma_start(out=wnat_s[1 * D:2 * D, :], in_=wiw[:, :]), "s_wn0", 16)
                    ti(g.dma_start(out=wnat_s[0 * D:1 * D, :], in_=wf[:, :]), "s_wn1", 16)
                    ti(g.dma_start(out=wnat_s[2 * D:3 * D, :], in_=wow[:, :]), "s_wn2", 16)
                    wr_src = wr[:, :].rearrange("(k p) e -> p k e", p=128)
                    ti(g.dma_start(out=wr_bf[:, :].rearrange("p (k e) -> p k e", k=KC),
                                   in_=wr_src), "s_wr", 16)
                    ti(g.dma_start(out=b1t_bf[:, :], in_=b1[:, :]), "gp_cst", 16)
                    for h in range(KC):
                        src = wo[:, h * 128:(h + 1) * 128].rearrange("(ic p) hh -> p ic hh", p=128)
                        dst = wos[:, h * I_FF:(h + 1) * I_FF].rearrange("p (ic hh) -> p ic hh", ic=IC)
                        ti(g.dma_start(out=dst, in_=src), "gp_cst", 16)
                    for grp in range(NG):
                        for k in range(KC):
                            src = w1[:, k * 128:(k + 1) * 128,
                                     grp * G * 128:(grp + 1) * G * 128].rearrange("e p i -> p e i")
                            off = (grp * KC + k) * E * G * 128
                            dst = w1s[:, off:off + E * G * 128].rearrange("p (e i) -> p e i", e=E)
                            ti(g.dma_start(out=dst, in_=src), "gp_cst", 16)
                    W(g, "gp_cst", 4240)
                    g.nop().then_inc(SM["bar"], 1)

                @blockA.sync
                def _(sy):
                    def load_x(t):
                        W(sy, "s_xfree", t - 1)
                        src = xin[t * TT:(t + 1) * TT, :].rearrange("(s p) d -> p s d", p=128)
                        dst = x_buf[t % 2][:, :].rearrange("p (s d) -> p s d", s=SUB)
                        ti(sy.dma_start(out=dst, in_=src), "s_x%d" % (t % 2), 16)

                    load_x(0)
                    load_x(1)
                    for dst_, src_ in [(norw_cm, norw), (gam_cm, gam), (bet_cm, bet),
                                       (bo_cm, bo)]:
                        srcb = src_[:].rearrange("(k p) -> p k", p=128)
                        ti(sy.dma_start(out=dst_[:, :], in_=srcb,
                                        allow_slow_non_contiguous=True), "s_bias", 16)
                    W(sy, "s_wn0", 16)
                    for k in range(KC):
                        ti(sy.dma_start_transpose(
                            wiT[:, k * D:(k + 1) * D],
                            wnat_s[1 * D:2 * D, k * 128:(k + 1) * 128]), "s_wt", 16)
                    W(sy, "s_wn1", 16)
                    for k in range(KC):
                        ti(sy.dma_start_transpose(
                            wfT[:, k * D:(k + 1) * D],
                            wnat_s[0 * D:1 * D, k * 128:(k + 1) * 128]), "s_wtB", 16)
                    W(sy, "s_wn2", 16)
                    for k in range(KC):
                        ti(sy.dma_start_transpose(
                            woT[:, k * D:(k + 1) * D],
                            wnat_s[2 * D:3 * D, k * 128:(k + 1) * 128]), "s_wt2", 16)
                    for t in range(NT):
                        if t + 2 < NT:
                            load_x(t + 2)
                        W(sy, "ss_xnev", t + 1)
                        ti(sy.dma_start(out=xnbf_s[t * 128:(t + 1) * 128, :],
                                        in_=xn_fm[t % 2][:, :]), "s_xnw%d" % (t % 2), 16)
                        W(sy, "sv_s", t + 1)
                        ti(sy.dma_start(out=ss_s[t * 128:(t + 1) * 128, :], in_=s_bf[:, :]),
                           "s_sw", 16)
                    W(sy, "s_xnw0", 16 * NT // 2)
                    W(sy, "s_xnw1", 16 * NT // 2)
                    W(sy, "s_sw", 16 * NT)
                    sy.nop().then_inc(SM["bar"], 1)

                @blockA.scalar
                def _(sc):

                    def accum_piece(t, i):
                        if i == 0:
                            W(sc, "s_x%d" % (t % 2), 16 * (t // 2 + 1))
                            W(sc, "sv_st", t)
                        sc.drain()
                        sub = i % SUB
                        fn = AF.Copy if i < SUB else AF.Square
                        sc.activation(junk[:, :], x_buf[t % 2][:, sub * D:(sub + 1) * D],
                                      fn, accum_out=stt[:, i:i + 1])
                        if i == 7:
                            DR(sc, "ss_acc")

                    def accums(t):
                        for i in range(8):
                            accum_piece(t, i)

                    def sqrtf(t):
                        W(sc, "sv_st", t + 1)
                        ti(sc.activation(stt[:, 40:48], stt[:, 32:40], AF.Sqrt), "ss_rr", 1)

                    accums(0)
                    sqrtf(0)
                    accums(1)
                    for t in range(NT):
                        if t + 1 < NT:
                            sqrtf(t + 1)
                        for c in range(KC):
                            r = 24 * t + 8 + c
                            b = r % 2
                            W(sc, "pj_R%d" % b, r // 2 + 1)
                            ti(sc.activation(f_fm[:, c * TT:(c + 1) * TT], pj[b][:, :],
                                             AF.Sigmoid), "pj_F%d" % b, 1)
                            if c % 2 == 1 and t + 2 < NT:
                                accum_piece(t + 2, c // 2)
                        for c in range(KC):
                            r = 24 * t + 16 + c
                            b = r % 2
                            W(sc, "pj_R%d" % b, r // 2 + 1)
                            ti(sc.activation(o_fm[:, c * TT:(c + 1) * TT], pj[b][:, :],
                                             AF.Sigmoid), "pj_F%d" % b, 1)
                            if c % 2 == 1 and t + 2 < NT:
                                accum_piece(t + 2, 4 + c // 2)
                        DR(sc, "ss_o")
                    DR(sc, "bar")

                @blockA.vector
                def _(v):
                    W(v, "s_prep", 1)
                    W(v, "s_bias", 64)
                    v.tensor_scalar(ident_bf[:, :], iota_f[:, :], pidx[:, :], None, OP.is_equal)
                    v.tensor_scalar(ident_f[:, :], iota_f[:, :], pidx[:, :], None, OP.is_equal)
                    v.tensor_scalar(sel_bf[:, :], sel_f[:, :], pidx[0:4, :], None, OP.is_equal)
                    DR(v, "s_sel")

                    def vstats(t):
                        W(v, "ss_acc", t + 1)
                        v.tensor_scalar(stt[:, 8:12], stt[:, 0:4], 1.0 / D, None, OP.mult)
                        v.tensor_scalar(stt[:, 12:16], stt[:, 4:8], 1.0 / D, None, OP.mult)
                        v.drain()
                        v.tensor_tensor(stt[:, 16:20], stt[:, 8:12], stt[:, 8:12], OP.mult)
                        v.drain()
                        v.tensor_tensor(stt[:, 20:24], stt[:, 12:16], stt[:, 16:20], OP.subtract)
                        v.drain()
                        v.tensor_scalar(stt[:, 24:28], stt[:, 20:24], 1.0, EPS1, OP.mult, OP.add)
                        v.tensor_scalar(stt[:, 28:32], stt[:, 20:24], 1.0 + EPS2, EPS1 * EPS2,
                                        OP.mult, OP.add)
                        v.drain()
                        v.reciprocal(stt[:, 32:40], stt[:, 24:32])
                        DR(v, "sv_st")

                    def xnts(t):
                        W(v, "ss_rr", t + 1)
                        W(v, "trA_R0", 4 * t)
                        W(v, "trA_R1", 4 * t)
                        W(v, "trB_R0", 4 * t)
                        W(v, "trB_R1", 4 * t)
                        for sub in range(SUB):
                            v.tensor_scalar(xn_tm[:, sub * D:(sub + 1) * D],
                                            x_buf[t % 2][:, sub * D:(sub + 1) * D],
                                            stt[:, 8 + sub:9 + sub], stt[:, 40 + sub:41 + sub],
                                            OP.subtract, OP.mult)
                        for sub in range(SUB):
                            v.tensor_scalar(xn2_tm[:, sub * D:(sub + 1) * D],
                                            x_buf[t % 2][:, sub * D:(sub + 1) * D],
                                            stt[:, 8 + sub:9 + sub], stt[:, 44 + sub:45 + sub],
                                            OP.subtract, OP.mult)
                        DR(v, "sv_xn", "s_xfree")

                    def trevacs(tau):
                        W(v, "s_xnw%d" % (tau % 2), 16 * (tau // 2))
                        W(v, "sv_s", tau - 1)
                        for k in range(KC):
                            g_ = 8 * tau + k
                            p_ = g_ % 2
                            W(v, "trA_R%d" % p_, g_ // 2 + 1)
                            ti(v.tensor_copy(xn_fm[tau % 2][:, k * TT:(k + 1) * TT],
                                             trA[p_][:, :]), "trA_F%d" % p_, 1)
                            W(v, "trB_R%d" % p_, g_ // 2 + 1)
                            ti(v.tensor_scalar(xn2_fm[:, k * TT:(k + 1) * TT], trB[p_][:, :],
                                               norw_cm[:, k:k + 1], None, OP.mult),
                               "trB_F%d" % p_, 1)
                        DR(v, "ss_xnev", "sv_x2ev")

                    vstats(0)
                    xnts(0)
                    vstats(1)
                    trevacs(0)
                    for t in range(NT):
                        for c in range(KC):
                            r = 24 * t + c
                            b = r % 2
                            W(v, "pj_R%d" % b, r // 2 + 1)
                            ti(v.tensor_copy(v_bf[:, c * TT:(c + 1) * TT], pj[b][:, :]),
                               "pj_F%d" % b, 1)
                        if t + 1 < NT:
                            xnts(t + 1)
                        for c in range(KC):
                            r = 24 * t + 8 + c
                            b = r % 2
                            W(v, "pj_F%d" % b, r // 2 + 1)
                            v.tensor_scalar(f_fm[:, c * TT:(c + 1) * TT],
                                            f_fm[:, c * TT:(c + 1) * TT], FORGET_LB, None,
                                            OP.max)
                        v.drain()
                        for h in range(KC):
                            b = h % 2
                            W(v, "vb_R%d" % b, 4 * t + h // 2 + 1)
                            ti(v.scalar_tensor_tensor(negu_bf[:, h * TT:(h + 1) * TT],
                                                      f_fm[:, h * TT:(h + 1) * TT], 1.0,
                                                      vbp[b][:, :], OP.subtract, OP.mult),
                               "vb_F%d" % b, 1)
                        v.drain()
                        if t % 4 != 0:
                            nS3 = negS_bf[:, :].rearrange("p (h tt) -> p h tt", h=KC)
                            v.tensor_copy(carry[:, :].rearrange("p (h o) -> p h o", o=1),
                                          nS3[:, :, TT - 1:TT])
                            v.drain()
                        for h in range(KC):
                            init = 0.0 if t % 4 == 0 else carry[:, h:h + 1]
                            v.tensor_tensor_scan(negS_bf[:, h * TT:(h + 1) * TT],
                                                 f_fm[:, h * TT:(h + 1) * TT],
                                                 negu_bf[:, h * TT:(h + 1) * TT],
                                                 init, OP.mult, OP.add)
                        if t + 2 < NT:
                            vstats(t + 2)
                        W(v, "s_lgR", t + 1)
                        lg3 = prt[:, 0:SUB * E].rearrange("p (s e) -> p s e", e=E)
                        mk3 = mask_all[:, t * SUB * E:(t + 1) * SUB * E].rearrange(
                            "p (s e) -> p s e", e=E)
                        st3 = stt[:, :].rearrange("p (c o) -> p c o", o=1)
                        for sub in range(SUB):
                            v.tensor_reduce(stt[:, 48 + sub:49 + sub], lg3[:, sub:sub + 1, :],
                                            mybir.AxisListType.X, OP.max)
                        v.drain()
                        rmax = st3[:, 48:52, :]
                        v.tensor_tensor(st3[:, 52:56, :], lg3[:, :, 0:1], rmax, OP.is_ge)
                        v.tensor_tensor(st3[:, 60:64, :], lg3[:, :, 1:2], rmax, OP.is_ge)
                        ins = v.tensor_tensor(st3[:, 64:68, :], lg3[:, :, 2:3], rmax, OP.is_ge)
                        ti(ins, "sv_lgF", 1)
                        v.drain()
                        v.tensor_scalar(st3[:, 56:60, :], st3[:, 52:56, :], -1.0, 1.0,
                                        OP.mult, OP.add)
                        v.tensor_copy(mk3[:, :, 0:1], st3[:, 52:56, :])
                        v.drain()
                        v.tensor_tensor(st3[:, 60:64, :], st3[:, 60:64, :], st3[:, 56:60, :],
                                        OP.mult)
                        v.drain()
                        v.tensor_tensor(st3[:, 56:60, :], st3[:, 56:60, :], st3[:, 60:64, :],
                                        OP.subtract)
                        v.tensor_copy(mk3[:, :, 1:2], st3[:, 60:64, :])
                        v.drain()
                        v.tensor_tensor(st3[:, 64:68, :], st3[:, 64:68, :], st3[:, 56:60, :],
                                        OP.mult)
                        v.drain()
                        v.tensor_copy(mk3[:, :, 2:3], st3[:, 64:68, :])
                        v.tensor_tensor(st3[:, 56:60, :], st3[:, 56:60, :], st3[:, 64:68, :],
                                        OP.subtract)
                        v.drain()
                        ti(v.tensor_copy(mk3[:, :, 3:4], st3[:, 56:60, :]), "sv_mtm", 1)
                        v.drain()
                        if t + 1 < NT:
                            trevacs(t + 1)
                        W(v, "ss_o", t + 1)
                        v.drain()
                        v.tensor_tensor(negu_bf[:, :], o_fm[:, :], negS_bf[:, :], OP.mult)
                        W(v, "s_sw", 16 * t)
                        W(v, "ss_xnev", t + 1)
                        v.drain()
                        ti(v.tensor_tensor(s_bf[:, :], xn_fm[t % 2][:, :], negu_bf[:, :],
                                           OP.subtract), "sv_s", 1)
                    DR(v, "bar")

                @blockA.tensor
                def _(te):
                    W(te, "s_sel", 1)
                    for t in range(NT):
                        W(te, "sv_xn", t + 1)
                        W(te, "sv_lgF", t)
                        for k in range(KC):
                            g_ = 8 * t + k
                            p_ = g_ % 2
                            W(te, "trA_F%d" % p_, g_ // 2)
                            ins = None
                            for sub in range(SUB):
                                ins = te.transpose(
                                    trA[p_][:, sub * 128:(sub + 1) * 128],
                                    xn_tm[:, sub * D + k * 128: sub * D + (k + 1) * 128],
                                    ident_bf[:, :])
                            ti(ins, "trA_R%d" % p_, 1)
                            W(te, "trB_F%d" % p_, g_ // 2)
                            for sub in range(SUB):
                                ins = te.transpose(
                                    trB[p_][:, sub * 128:(sub + 1) * 128],
                                    xn2_tm[:, sub * D + k * 128: sub * D + (k + 1) * 128],
                                    ident_bf[:, :])
                            ti(ins, "trB_R%d" % p_, 1)
                        W(te, "ss_xnev", t + 1)
                        if t == 0:
                            W(te, "s_wr", 16)
                        ins = None
                        for sub in range(SUB):
                            for k in range(KC):
                                ins = te.matmul(
                                    prt[:, sub * E:(sub + 1) * E],
                                    xn_fm[t % 2][:, k * TT + sub * 128: k * TT + (sub + 1) * 128],
                                    wr_bf[:, k * E:(k + 1) * E],
                                    start=(k == 0), stop=(k == KC - 1))
                        ti(ins, "s_lgR", 1)
                        W(te, "sv_x2ev", t + 1)
                        for pi, wT in enumerate([wiT, wfT, woT]):
                            if t == 0:
                                if pi == 0:
                                    W(te, "s_wt", 16 * 8)
                                elif pi == 1:
                                    W(te, "s_wtB", 16 * 8)
                                else:
                                    W(te, "s_wt2", 16 * 8)
                            for c in range(KC):
                                r = 24 * t + pi * KC + c
                                b = r % 2
                                W(te, "pj_F%d" % b, r // 2)
                                ins = None
                                for k in range(KC):
                                    ins = te.matmul(pj[b][:, :],
                                                    wT[:, k * D + c * 128: k * D + (c + 1) * 128],
                                                    xn2_fm[:, k * TT:(k + 1) * TT],
                                                    start=(k == 0), stop=(k == KC - 1))
                                ti(ins, "pj_R%d" % b, 1)
                            if pi == 1:
                                for h in range(KC):
                                    b = h % 2
                                    r = 24 * t + h
                                    W(te, "pj_F%d" % (r % 2), r // 2 + 1)
                                    W(te, "vb_F%d" % b, 4 * t + h // 2)
                                    ti(te.matmul(vbp[b][:, :], ones_bf[:, :],
                                                 v_bf[:, h * TT:(h + 1) * TT],
                                                 start=True, stop=True), "vb_R%d" % b, 1)
                    DR(te, "bar")

        with contextlib.ExitStack() as esB:
            EB = esB.enter_context
            w1sl = [EB(nc.sbuf_tensor("w1sl0", [128, KC * E * G * 128], BF16)),
                    EB(nc.sbuf_tensor("w1sl1", [128, KC * E * G * 128], BF16))]
            wosl = [EB(nc.sbuf_tensor("wosl0", [128, IC * 128], BF16)),
                    EB(nc.sbuf_tensor("wosl1", [128, IC * 128], BF16))]
            maskedA = EB(nc.sbuf_tensor("maskedA", [128, E * KC * TT], BF16))
            interA = EB(nc.sbuf_tensor("interA", [128, IC * TT], BF16))
            xn_b = [EB(nc.sbuf_tensor("xn_b0", [128, KC * TT], BF16)),
                    EB(nc.sbuf_tensor("xn_b1", [128, KC * TT], BF16))]
            s_b = EB(nc.sbuf_tensor("s_b", [128, KC * TT], BF16))
            y3 = EB(nc.sbuf_tensor("y3", [128, KC * TT], F32))
            g_bf = EB(nc.sbuf_tensor("g_bf", [128, KC * TT], BF16))
            mask4_bf = EB(nc.sbuf_tensor("mask4_bf", [4, TT], BF16))
            mu_rep = EB(nc.sbuf_tensor("mu_rep", [128, TT], F32))
            rstd_rep = EB(nc.sbuf_tensor("rstd_rep", [128, TT], F32))
            mb_bf = EB(nc.sbuf_tensor("mb_bf", [128, E * TT], BF16))
            jrow = EB(nc.sbuf_tensor("jrow", [128, TT], F32))
            p_bf = EB(nc.sbuf_tensor("p_bf", [128, KC * TT], BF16))


            out_tm = y3

            pre = [EB(nc.psum_tensor("pre0", [128, TT], F32)),
                   EB(nc.psum_tensor("pre1", [128, TT], F32))]
            y2 = [EB(nc.psum_tensor("y2a", [128, TT], F32)),
                  EB(nc.psum_tensor("y2b", [128, TT], F32))]
            pmu = EB(nc.psum_tensor("pmu", [128, TT], F32))
            psq = EB(nc.psum_tensor("psq", [128, TT], F32))
            pmb = EB(nc.psum_tensor("pmb", [128, TT], F32))
            pmb2 = EB(nc.psum_tensor("pmb2", [128, TT], F32))
            pmb_bf = pmb.bitcast(BF16)
            pmb2_bf = pmb2.bitcast(BF16)
            pmu_bf = pmu.bitcast(BF16)
            psq_bf = psq.bitcast(BF16)

            y3bf = p_bf
            y3sq = g_bf

            def gsl(k0, k1):
                return g_bf[:, k0 * TT:k1 * TT]

            with nc.Block() as blockB:

                @blockB.sync
                def _(sy):
                    W(sy, "bar", 5)
                    ti(sy.dma_start(out=xn_b[0][:, :], in_=xnbf_s[0:128, :]), "s_xb0", 16)
                    for t in range(NT):
                        W(sy, "s_xfree", t + 7)
                        src = xin[t * TT:(t + 1) * TT, :].rearrange("(s p) d -> p s d", p=128)
                        dst = x_buf[t % 2][:, :].rearrange("p (s d) -> p s d", s=SUB)
                        ti(sy.dma_start(out=dst, in_=src), "s_x%d" % (t % 2), 16)
                        if t + 1 < NT:
                            W(sy, "y2_R1", 4 * t)
                            W(sy, "sv_msk", t)
                            ti(sy.dma_start(out=xn_b[(t + 1) % 2][:, :],
                                            in_=xnbf_s[(t + 1) * 128:(t + 2) * 128, :]),
                               "s_xb%d" % ((t + 1) % 2), 16)
                        for grp in range(NG):
                            gi = t * NG + grp
                            if gi == 0:
                                W(sy, "gp_cst", 4240)
                            W(sy, "sl_F", gi - 1)
                            ti(sy.dma_start(out=w1sl[gi % 2][:, :],
                                            in_=w1s[:, grp * 4096:(grp + 1) * 4096]),
                               "sl_R%d" % (gi % 2), 16)
                        for h in range(KC):
                            wi_ = t * KC + h
                            W(sy, "wo_F", wi_ - 1)
                            ti(sy.dma_start(out=wosl[wi_ % 2][:, :],
                                            in_=wos[:, h * I_FF:(h + 1) * I_FF]),
                               "wo_R%d" % (wi_ % 2), 16)
                        W(sy, "sv_p", t)
                        ti(sy.dma_start(out=s_b[:, :], in_=ss_s[t * 128:(t + 1) * 128, :]),
                           "s_sb", 16)
                    sy.wait_ge(SM["st_out"], 16 * NT)

                @blockB.gpsimd
                def _(g):
                    W(g, "bar", 5)
                    for t in range(NT):
                        W(g, "sv_out", t + 1)
                        W(g, "st_out", 16 * t)
                        dsto = out[t * TT:(t + 1) * TT, :].rearrange("(s p) d -> p s d", p=128)
                        srco = out_tm[:, :].rearrange("p (s d) -> p s d", s=SUB)
                        ti(g.dma_start(out=dsto, in_=srco), "st_out", 16)
                    g.wait_ge(SM["st_out"], 16 * NT)

                @blockB.tensor
                def _(te):
                    W(te, "bar", 5)

                    def emit_ptr(tau):
                        W(te, "sv_p", tau + 1)
                        for sub in range(SUB):
                            W(te, "mu_F", 5 * tau + sub + 1)
                            ins = None
                            for kk in range(4):
                                ins = te.transpose(
                                    pmu_bf[:, kk * 128:(kk + 1) * 128],
                                    p_bf[:, kk * TT + sub * 128: kk * TT + (sub + 1) * 128],
                                    ident_bf[:, :])
                            ti(ins, "mu_R", 1)
                            W(te, "sq_F", 5 * tau + sub + 1)
                            for kk in range(4):
                                ins = te.transpose(
                                    psq_bf[:, kk * 128:(kk + 1) * 128],
                                    p_bf[:, (4 + kk) * TT + sub * 128: (4 + kk) * TT + (sub + 1) * 128],
                                    ident_bf[:, :])
                            ti(ins, "sq_R", 1)

                    def mask_prep(tau):
                        W(te, "ss_mb", 2 * tau)
                        ins = None
                        for sub in range(SUB):
                            ins = te.transpose(
                                pmb2_bf[0:4, sub * 128:(sub + 1) * 128],
                                mask_all[:, (tau * SUB + sub) * E:(tau * SUB + sub + 1) * E],
                                ident_bf[:, :])
                        ti(ins, "s_mtR", 1)
                        W(te, "sv_m4", tau + 1)
                        te.matmul(pmb[:, :], sel_bf[:, 0:128], mask4_bf[:, :],
                                  start=True, stop=True)
                        ti(te.matmul(pmb2[:, :], sel_bf[:, 128:256], mask4_bf[:, :],
                                     start=True, stop=True), "s_mbR", 1)
                        W(te, "ss_mb", 2 * tau + 1)
                        te.matmul(pmb[:, :], sel_bf[:, 256:384], mask4_bf[:, :],
                                  start=True, stop=True)
                        ti(te.matmul(pmb2[:, :], sel_bf[:, 384:512], mask4_bf[:, :],
                                     start=True, stop=True), "s_mbR", 1)

                    mask_prep(0)
                    for t in range(NT):
                        W(te, "sv_msk", t + 1)
                        for grp in range(NG):
                            if grp == NG // 2 and t > 0:
                                emit_ptr(t - 1)
                            gi = t * NG + grp
                            W(te, "sl_R%d" % (gi % 2), 16 * (gi // 2 + 1))
                            for il in range(G):
                                i_abs = grp * G + il
                                b = i_abs % 2
                                pr = t * IC + i_abs
                                W(te, "pre_F%d" % b, pr // 2)
                                first = True
                                for e in range(E):
                                    for k in range(KC):
                                        off = ((k * E + e) * G + il) * 128
                                        ins = te.matmul(
                                            pre[b][:, :], w1sl[gi % 2][:, off:off + 128],
                                            maskedA[:, (e * KC + k) * TT:(e * KC + k + 1) * TT],
                                            start=first, stop=False)
                                        first = False
                                        if il == G - 1 and e == E - 1 and k == KC - 1:
                                            ti(ins, "sl_F", 1)
                                ti(te.matmul(pre[b][:, :],
                                             b1t_bf[:, i_abs * 128:(i_abs + 1) * 128],
                                             mask4_bf[:, :], start=False, stop=True),
                                   "pre_R%d" % b, 1)
                        if t + 1 < NT:
                            mask_prep(t + 1)
                        W(te, "ss_int", t + 1)
                        for h in range(KC):
                            wi_ = t * KC + h
                            W(te, "wo_R%d" % (wi_ % 2), 16 * (wi_ // 2 + 1))
                            b = h % 2
                            W(te, "y2_F%d" % b, wi_ // 2)
                            ins = None
                            for ic in range(IC):
                                ins = te.matmul(y2[b][:, :],
                                                wosl[wi_ % 2][:, ic * 128:(ic + 1) * 128],
                                                interA[:, ic * TT:(ic + 1) * TT],
                                                start=(ic == 0), stop=False)
                            ti(ins, "wo_F", 1)
                            ti(te.matmul(y2[b][:, :], ident_bf[:, :],
                                         xn_b[t % 2][:, h * TT:(h + 1) * TT],
                                         start=False, stop=True), "y2_R%d" % b, 1)
                        W(te, "sv_y3bf", t + 1)
                        W(te, "mu_F", 5 * t)
                        W(te, "sq_F", 5 * t)
                        ins = None
                        for k in range(KC):
                            ins = te.matmul(pmu[:, :], ones_bf[:, :],
                                            y3bf[:, k * TT:(k + 1) * TT],
                                            start=(k == 0), stop=(k == KC - 1))
                        ti(ins, "mu_R", 1)
                        for k in range(KC):
                            ins = te.matmul(psq[:, :], ones_bf[:, :],
                                            y3sq[:, k * TT:(k + 1) * TT],
                                            start=(k == 0), stop=(k == KC - 1))
                        ti(ins, "sq_R", 1)
                    emit_ptr(NT - 1)

                @blockB.vector
                def _(v):
                    W(v, "bar", 5)

                    def final_adds(tau):
                        W(v, "s_x%d" % (tau % 2), 16 * (NT // 2 + tau // 2 + 1))
                        for sub in range(SUB):
                            W(v, "mu_R", 5 * tau + 2 + sub)
                            ti(v.tensor_tensor(out_tm[:, sub * D: sub * D + TT],
                                               x_buf[tau % 2][:, sub * D: sub * D + TT],
                                               pmu_bf[:, 0:TT], OP.add), "mu_F", 1)
                            W(v, "sq_R", 5 * tau + 2 + sub)
                            ti(v.tensor_tensor(out_tm[:, sub * D + TT: (sub + 1) * D],
                                               x_buf[tau % 2][:, sub * D + TT: (sub + 1) * D],
                                               psq_bf[:, 0:TT], OP.add), "sq_F", 1)
                        DR(v, "sv_out", "s_xfree")

                    def tail(tau):
                        W(v, "mu_R", 5 * tau + 1)
                        W(v, "sq_R", 5 * tau + 1)
                        ti(v.tensor_scalar(mu_rep[:, :], pmu[:, :], 1.0 / D, None, OP.mult),
                           "mu_F", 1)
                        ti(v.tensor_scalar(rstd_rep[:, :], psq[:, :], 1.0 / D, None, OP.mult),
                           "sq_F", 1)
                        v.drain()
                        v.tensor_tensor(jrow[:, :], mu_rep[:, :], mu_rep[:, :], OP.mult)
                        v.drain()
                        v.scalar_tensor_tensor(rstd_rep[:, :], rstd_rep[:, :], EPS3,
                                               jrow[:, :], OP.add, OP.subtract)
                        v.drain()
                        v.reciprocal(jrow[:, :], rstd_rep[:, :])
                        DR(v, "sv_var")
                        W(v, "ss_sqrt", tau + 1)
                        for k in range(KC):
                            v.tensor_tensor(y3[:, k * TT:(k + 1) * TT],
                                            y3[:, k * TT:(k + 1) * TT], mu_rep[:, :],
                                            OP.subtract)
                        v.drain()
                        for k in range(KC):
                            v.tensor_tensor(y3[:, k * TT:(k + 1) * TT],
                                            y3[:, k * TT:(k + 1) * TT], rstd_rep[:, :],
                                            OP.mult)
                        DR(v, "sv_z")
                        W(v, "ss_g", 8 * (tau + 1))
                        W(v, "s_sb", 16 * (tau + 1))
                        ti(v.tensor_tensor(p_bf[:, :], s_b[:, :], gsl(0, KC), OP.mult),
                           "sv_p", 1)

                    def mask_block(tau):
                        W(v, "s_mtR", tau + 1)
                        W(v, "pre_R1", 16 * tau)
                        ti(v.tensor_copy(mask4_bf[:, :], pmb2_bf[0:4, 0:TT]), "sv_m4", 1)
                        W(v, "ss_mb", 2 * (tau + 1))
                        W(v, "s_xb%d" % (tau % 2), 16 * (tau // 2 + 1))
                        ins = None
                        for e in range(E):
                            for k in range(KC):
                                ins = v.tensor_tensor(
                                    maskedA[:, (e * KC + k) * TT:(e * KC + k + 1) * TT],
                                    xn_b[tau % 2][:, k * TT:(k + 1) * TT],
                                    mb_bf[:, e * TT:(e + 1) * TT], OP.mult)
                        DR(v, "sv_msk")

                    mask_block(0)
                    for t in range(NT):
                        if t > 0:
                            tail(t - 1)
                            final_adds(t - 1)
                        if t + 1 < NT:
                            mask_block(t + 1)
                        W(v, "mu_R", 5 * t)
                        W(v, "sq_R", 5 * t)
                        for h in range(KC):
                            W(v, "y2_F%d" % (h % 2), (t * KC + h) // 2 + 1)
                            v.tensor_copy(y3bf[:, h * TT:(h + 1) * TT],
                                          y3[:, h * TT:(h + 1) * TT])
                            v.tensor_tensor(y3sq[:, h * TT:(h + 1) * TT],
                                            y3[:, h * TT:(h + 1) * TT],
                                            y3[:, h * TT:(h + 1) * TT], OP.mult)
                        DR(v, "sv_y3bf")
                    tail(NT - 1)
                    final_adds(NT - 1)

                @blockB.scalar
                def _(sc):
                    W(sc, "bar", 5)

                    def sqrtp(tau):
                        W(sc, "sv_var", tau + 1)
                        ti(sc.activation(rstd_rep[:, :], jrow[:, :], AF.Sqrt), "ss_sqrt", 1)

                    def silup(tau):
                        W(sc, "sv_z", tau + 1)
                        for k in range(KC):
                            ti(sc.activation(gsl(k, k + 1), y3[:, k * TT:(k + 1) * TT],
                                             AF.Silu, bias=bet_cm[:, k:k + 1],
                                             scale=gam_cm[:, k:k + 1]), "ss_g", 1)

                    def mb_evacs(tau):
                        W(sc, "s_mbR", 2 * tau + 1)
                        sc.copy(mb_bf[:, 0:TT], pmb[:, :])
                        ti(sc.copy(mb_bf[:, TT:2 * TT], pmb2[:, :]), "ss_mb", 1)
                        W(sc, "s_mbR", 2 * tau + 2)
                        sc.copy(mb_bf[:, 2 * TT:3 * TT], pmb[:, :])
                        ti(sc.copy(mb_bf[:, 3 * TT:4 * TT], pmb2[:, :]), "ss_mb", 1)

                    mb_evacs(0)
                    for t in range(NT):
                        sc.drain()
                        W(sc, "y2_R1", 4 * t)
                        for i_abs in range(IC):
                            if i_abs == 4 and t > 0:
                                sqrtp(t - 1)
                            if i_abs == 12 and t > 0:
                                silup(t - 1)
                            b = i_abs % 2
                            pr = t * IC + i_abs
                            W(sc, "pre_R%d" % b, pr // 2 + 1)
                            ti(sc.activation(interA[:, i_abs * TT:(i_abs + 1) * TT],
                                             pre[b][:, :], AF.Gelu), "pre_F%d" % b, 1)
                        DR(sc, "ss_int")
                        if t + 1 < NT:
                            mb_evacs(t + 1)
                        W(sc, "st_out", 16 * t)
                        for h in range(KC):
                            wi_ = t * KC + h
                            b = h % 2
                            W(sc, "y2_R%d" % b, wi_ // 2 + 1)
                            ti(sc.activation(y3[:, h * TT:(h + 1) * TT], y2[b][:, :],
                                             AF.Identity, bias=bo_cm[:, h:h + 1]),
                               "y2_F%d" % b, 1)
                        DR(sc, "ss_y3")
                    sqrtp(NT - 1)
                    silup(NT - 1)


    return nc


_NC_CACHE = {}


def kernel(L0_attn_x, L0_attn_norm_w, L0_attn_w_forget, L0_attn_w_input,
           L0_attn_w_output, L0_ffn_W_router, L0_ffn_W1, L0_ffn_b1, L0_ffn_Wo,
           L0_ffn_bo, L0_ffn_ln_gamma, L0_ffn_ln_beta):
    x = np.ascontiguousarray(np.asarray(L0_attn_x, dtype=np.float32))
    weights = {
        "wf": np.ascontiguousarray(np.asarray(L0_attn_w_forget, np.float32)),
        "wiw": np.ascontiguousarray(np.asarray(L0_attn_w_input, np.float32)),
        "wow": np.ascontiguousarray(np.asarray(L0_attn_w_output, np.float32)),
        "norw": np.ascontiguousarray(np.asarray(L0_attn_norm_w, np.float32)),
        "wr": np.ascontiguousarray(np.asarray(L0_ffn_W_router, np.float32)),
        "w1": np.ascontiguousarray(np.asarray(L0_ffn_W1, np.float32)),
        "b1": np.ascontiguousarray(np.asarray(L0_ffn_b1, np.float32)),
        "wo": np.ascontiguousarray(np.asarray(L0_ffn_Wo, np.float32)),
        "bo": np.ascontiguousarray(np.asarray(L0_ffn_bo, np.float32)),
        "gam": np.ascontiguousarray(np.asarray(L0_ffn_ln_gamma, np.float32)),
        "bet": np.ascontiguousarray(np.asarray(L0_ffn_ln_beta, np.float32)),
    }
    if "nc" not in _NC_CACHE:
        _NC_CACHE["nc"] = build_core()
    nc = _NC_CACHE["nc"]
    in_maps = []
    for c in range(8):
        shard = x[2 * c:2 * c + 2].reshape(TCORE, D)
        m = {"x": np.ascontiguousarray(shard)}
        m.update(weights)
        in_maps.append(m)
    res = run_bass_kernel_spmd(nc, in_maps, core_ids=list(range(8)))
    outs = [np.asarray(r["out"]).reshape(2, 2048, D) for r in res.results]
    return np.concatenate(outs, axis=0).astype(np.float32)


# version 19
# speedup vs baseline: 1.0138x; 1.0131x over previous
import contextlib
import numpy as np
import concourse.bass as bass
import concourse.mybir as mybir
from concourse.bass_utils import run_bass_kernel_spmd

F32 = mybir.dt.float32
BF16 = mybir.dt.bfloat16
AF = mybir.ActivationFunctionType
OP = mybir.AluOpType

D = 1024
KC = 8
TCORE = 4096
TT = 512
NT = 8
SUB = 4
I_FF = 4096
IC = 32
E = 4
G = 1
NG = IC // G
EPS1 = 1e-5
EPS2 = 1e-6
EPS3 = 1e-12
FORGET_LB = 0.9


def build_core(debug=False):
    nc = bass.Bass("TRN2", target_bir_lowering=False)

    xin = nc.dram_tensor("x", [TCORE, D], F32, kind="ExternalInput").ap()
    wf = nc.dram_tensor("wf", [D, D], F32, kind="ExternalInput").ap()
    wiw = nc.dram_tensor("wiw", [D, D], F32, kind="ExternalInput").ap()
    wow = nc.dram_tensor("wow", [D, D], F32, kind="ExternalInput").ap()
    norw = nc.dram_tensor("norw", [D], F32, kind="ExternalInput").ap()
    wr = nc.dram_tensor("wr", [D, E], F32, kind="ExternalInput").ap()
    w1 = nc.dram_tensor("w1", [E, D, I_FF], F32, kind="ExternalInput").ap()
    b1 = nc.dram_tensor("b1", [E, I_FF], F32, kind="ExternalInput").ap()
    wo = nc.dram_tensor("wo", [I_FF, D], F32, kind="ExternalInput").ap()
    bo = nc.dram_tensor("bo", [D], F32, kind="ExternalInput").ap()
    gam = nc.dram_tensor("gam", [D], F32, kind="ExternalInput").ap()
    bet = nc.dram_tensor("bet", [D], F32, kind="ExternalInput").ap()
    out = nc.dram_tensor("out", [TCORE, D], F32, kind="ExternalOutput").ap()

    sk = "ExternalOutput" if debug else "Internal"
    wnat_s = nc.dram_tensor("wnat_s", [3 * D, D], BF16, kind="Internal").ap()
    w1s = nc.dram_tensor("w1s", [128, 131072], BF16, kind="Internal").ap()
    wos = nc.dram_tensor("wos", [128, 32768], BF16, kind="Internal").ap()
    xnbf_s = nc.dram_tensor("xnbf_s", [NT * 128, KC * TT], BF16, kind=sk).ap()
    ss_s = nc.dram_tensor("ss_s", [NT * 128, KC * TT], BF16, kind=sk).ap()

    es = contextlib.ExitStack()
    with es:
        AS = nc.alloc_sbuf_tensor
        ident_bf = AS("ident_bf", [128, 128], BF16)
        ident_f = AS("ident_f", [128, 128], F32)
        ones_bf = AS("ones_bf", [128, 128], BF16)
        pidx = AS("pidx", [128, 1], F32)
        sel_bf = AS("sel_bf", [4, E * 128], BF16)
        wr_bf = AS("wr_bf", [128, KC * E], BF16)
        b1t_bf = AS("b1t_bf", [4, I_FF], BF16)
        norw_cm = AS("norw_cm", [128, KC], F32)
        gam_cm = AS("gam_cm", [128, KC], F32)
        bet_cm = AS("bet_cm", [128, KC], F32)
        bo_cm = AS("bo_cm", [128, KC], F32)
        stt = AS("stt", [128, 80], F32)
        mask_all = AS("mask_all", [128, NT * SUB * E], BF16)
        x_buf = [AS("x_a", [128, SUB * D], F32), AS("x_b", [128, SUB * D], F32)]

        sem_names = [
            "s_x0", "s_x1", "s_xfree", "gp_cst", "s_wn0", "s_wn1", "s_wn2", "s_wr", "s_bias", "s_wt", "s_wtB", "s_wt2", "s_prep", "s_sel",
            "ss_acc", "sv_st", "ss_rr", "sv_xn", "s_xnw0", "s_xnw1", "s_sw",
            "trA_R0", "trA_R1", "trA_F0", "trA_F1", "trB_R0", "trB_R1", "trB_F0", "trB_F1", "ss_xnev", "sv_x2ev",
            "pj_R0", "pj_F0", "pj_R1", "pj_F1",
            "ss_o", "vb_R0", "vb_F0", "vb_R1", "vb_F1",
            "sv_s", "bar",
            "s_xb0", "s_xb1", "s_sb", "sl_R0", "sl_R1", "sl_R2", "sl_F", "wo_R0", "wo_R1", "wo_F",
            "s_lgR", "sv_lgF", "sv_mtm", "s_mtR", "sv_m4", "s_mbR", "ss_mb", "sv_msk",
            "pre_R0", "pre_F0", "pre_R1", "pre_F1", "pre_R2", "pre_F2", "pre_R3", "pre_F3", "s_predone", "ss_int",
            "y2_R0", "y2_F0", "y2_R1", "y2_F1", "s_wodone", "ss_y3",
            "sv_y3bf", "mu_R", "mu_F", "sq_R", "sq_F", "sv_var", "ss_sqrt",
            "sv_z", "ss_g", "sv_p", "s_pdone", "sv_out", "st_out",
        ]
        SM = {n: es.enter_context(nc.semaphore(n)) for n in sem_names}

        def ti(instr, name, k=1):
            instr.then_inc(SM[name], k)
            return instr

        def W(eng, name, val):
            if val > 0:
                eng.wait_ge(SM[name], val)

        def DR(eng, *names):
            ins = eng.drain()
            if names:
                ins.then_inc(SM[names[0]], 1)
                for n in names[1:]:
                    eng.nop().then_inc(SM[n], 1)

        with contextlib.ExitStack() as esA:
            EA = esA.enter_context
            iota_f = EA(nc.sbuf_tensor("iota_f", [128, 128], F32))
            sel_f = EA(nc.sbuf_tensor("sel_f", [4, E * 128], F32))
            junk = EA(nc.sbuf_tensor("junk", [128, 1024], F32))
            carry = EA(nc.sbuf_tensor("carry", [128, KC], F32))
            wfT = EA(nc.sbuf_tensor("wfT", [128, KC * D], BF16))
            wiT = EA(nc.sbuf_tensor("wiT", [128, KC * D], BF16))
            woT = EA(nc.sbuf_tensor("woT", [128, KC * D], BF16))
            xn_tm = EA(nc.sbuf_tensor("xn_tm", [128, SUB * D], BF16))
            xn2_tm = EA(nc.sbuf_tensor("xn2_tm", [128, SUB * D], BF16))
            xn_fm = [EA(nc.sbuf_tensor("xn_fm0", [128, KC * TT], BF16)),
                     EA(nc.sbuf_tensor("xn_fm1", [128, KC * TT], BF16))]
            xn2_fm = EA(nc.sbuf_tensor("xn2_fm", [128, KC * TT], BF16))
            f_fm = EA(nc.sbuf_tensor("f_fm", [128, KC * TT], F32))
            o_fm = EA(nc.sbuf_tensor("o_fm", [128, KC * TT], F32))
            negu_bf = EA(nc.sbuf_tensor("negu_bf", [128, KC * TT], BF16))
            negS_bf = EA(nc.sbuf_tensor("negS_bf", [128, KC * TT], BF16))
            v_bf = EA(nc.sbuf_tensor("v_bf", [128, KC * TT], BF16))
            s_bf = EA(nc.sbuf_tensor("s_bf", [128, KC * TT], BF16))
            trA = [EA(nc.psum_tensor("trA0", [128, TT], BF16)),
                   EA(nc.psum_tensor("trA1", [128, TT], BF16))]
            trB = [EA(nc.psum_tensor("trB0", [128, TT], BF16)),
                   EA(nc.psum_tensor("trB1", [128, TT], BF16))]
            pj = [EA(nc.psum_tensor("pj0", [128, TT], F32)),
                  EA(nc.psum_tensor("pj1", [128, TT], F32))]
            vbp = [EA(nc.psum_tensor("vb0", [128, TT], F32)),
                   EA(nc.psum_tensor("vb1", [128, TT], F32))]
            prt = trA[0].bitcast(F32)
            wTs = [wfT, wiT, woT]

            with nc.Block() as blockA:

                @blockA.gpsimd
                def _(g):
                    g.memset(ones_bf[:, :], 1.0)
                    g.iota(iota_f[:, :], [[1, 128]], channel_multiplier=0,
                           allow_small_or_imprecise_dtypes=True)
                    g.iota(sel_f[:, :].rearrange("p (e j) -> p e j", e=E), [[1, E], [0, 128]],
                           channel_multiplier=0, allow_small_or_imprecise_dtypes=True)
                    g.iota(pidx[:, :], [[1, 1]], channel_multiplier=1,
                           allow_small_or_imprecise_dtypes=True)
                    DR(g, "s_prep")
                    ti(g.dma_start(out=wnat_s[1 * D:2 * D, :], in_=wiw[:, :]), "s_wn0", 16)
                    ti(g.dma_start(out=wnat_s[0 * D:1 * D, :], in_=wf[:, :]), "s_wn1", 16)
                    ti(g.dma_start(out=wnat_s[2 * D:3 * D, :], in_=wow[:, :]), "s_wn2", 16)
                    wr_src = wr[:, :].rearrange("(k p) e -> p k e", p=128)
                    ti(g.dma_start(out=wr_bf[:, :].rearrange("p (k e) -> p k e", k=KC),
                                   in_=wr_src), "s_wr", 16)
                    ti(g.dma_start(out=b1t_bf[:, :], in_=b1[:, :]), "gp_cst", 16)
                    for h in range(KC):
                        src = wo[:, h * 128:(h + 1) * 128].rearrange("(ic p) hh -> p ic hh", p=128)
                        dst = wos[:, h * I_FF:(h + 1) * I_FF].rearrange("p (ic hh) -> p ic hh", ic=IC)
                        ti(g.dma_start(out=dst, in_=src), "gp_cst", 16)
                    for grp in range(NG):
                        for k in range(KC):
                            src = w1[:, k * 128:(k + 1) * 128,
                                     grp * G * 128:(grp + 1) * G * 128].rearrange("e p i -> p e i")
                            off = (grp * KC + k) * E * G * 128
                            dst = w1s[:, off:off + E * G * 128].rearrange("p (e i) -> p e i", e=E)
                            ti(g.dma_start(out=dst, in_=src), "gp_cst", 16)
                    W(g, "gp_cst", 4240)
                    g.nop().then_inc(SM["bar"], 1)

                @blockA.sync
                def _(sy):
                    def load_x(t):
                        W(sy, "s_xfree", t - 1)
                        src = xin[t * TT:(t + 1) * TT, :].rearrange("(s p) d -> p s d", p=128)
                        dst = x_buf[t % 2][:, :].rearrange("p (s d) -> p s d", s=SUB)
                        ti(sy.dma_start(out=dst, in_=src), "s_x%d" % (t % 2), 16)

                    load_x(0)
                    load_x(1)
                    for dst_, src_ in [(norw_cm, norw), (gam_cm, gam), (bet_cm, bet),
                                       (bo_cm, bo)]:
                        srcb = src_[:].rearrange("(k p) -> p k", p=128)
                        ti(sy.dma_start(out=dst_[:, :], in_=srcb,
                                        allow_slow_non_contiguous=True), "s_bias", 16)
                    W(sy, "s_wn0", 16)
                    for k in range(KC):
                        ti(sy.dma_start_transpose(
                            wiT[:, k * D:(k + 1) * D],
                            wnat_s[1 * D:2 * D, k * 128:(k + 1) * 128]), "s_wt", 16)
                    W(sy, "s_wn1", 16)
                    for k in range(KC):
                        ti(sy.dma_start_transpose(
                            wfT[:, k * D:(k + 1) * D],
                            wnat_s[0 * D:1 * D, k * 128:(k + 1) * 128]), "s_wtB", 16)
                    W(sy, "s_wn2", 16)
                    for k in range(KC):
                        ti(sy.dma_start_transpose(
                            woT[:, k * D:(k + 1) * D],
                            wnat_s[2 * D:3 * D, k * 128:(k + 1) * 128]), "s_wt2", 16)
                    for t in range(NT):
                        if t + 2 < NT:
                            load_x(t + 2)
                        W(sy, "ss_xnev", t + 1)
                        ti(sy.dma_start(out=xnbf_s[t * 128:(t + 1) * 128, :],
                                        in_=xn_fm[t % 2][:, :]), "s_xnw%d" % (t % 2), 16)
                        W(sy, "sv_s", t + 1)
                        ti(sy.dma_start(out=ss_s[t * 128:(t + 1) * 128, :], in_=s_bf[:, :]),
                           "s_sw", 16)
                    W(sy, "s_xnw0", 16 * NT // 2)
                    W(sy, "s_xnw1", 16 * NT // 2)
                    W(sy, "s_sw", 16 * NT)
                    sy.nop().then_inc(SM["bar"], 1)

                @blockA.scalar
                def _(sc):

                    def accum_piece(t, i):
                        if i == 0:
                            W(sc, "s_x%d" % (t % 2), 16 * (t // 2 + 1))
                            W(sc, "sv_st", t)
                        sc.drain()
                        sub = i % SUB
                        fn = AF.Copy if i < SUB else AF.Square
                        sc.activation(junk[:, :], x_buf[t % 2][:, sub * D:(sub + 1) * D],
                                      fn, accum_out=stt[:, i:i + 1])
                        if i == 7:
                            DR(sc, "ss_acc")

                    def accums(t):
                        for i in range(8):
                            accum_piece(t, i)

                    def sqrtf(t):
                        W(sc, "sv_st", t + 1)
                        ti(sc.activation(stt[:, 40:48], stt[:, 32:40], AF.Sqrt), "ss_rr", 1)

                    accums(0)
                    sqrtf(0)
                    accums(1)
                    for t in range(NT):
                        if t + 1 < NT:
                            sqrtf(t + 1)
                        for c in range(KC):
                            r = 24 * t + 8 + c
                            b = r % 2
                            W(sc, "pj_R%d" % b, r // 2 + 1)
                            ti(sc.activation(f_fm[:, c * TT:(c + 1) * TT], pj[b][:, :],
                                             AF.Sigmoid), "pj_F%d" % b, 1)
                            if c % 2 == 1 and t + 2 < NT:
                                accum_piece(t + 2, c // 2)
                        for c in range(KC):
                            r = 24 * t + 16 + c
                            b = r % 2
                            W(sc, "pj_R%d" % b, r // 2 + 1)
                            ti(sc.activation(o_fm[:, c * TT:(c + 1) * TT], pj[b][:, :],
                                             AF.Sigmoid), "pj_F%d" % b, 1)
                            if c % 2 == 1 and t + 2 < NT:
                                accum_piece(t + 2, 4 + c // 2)
                        DR(sc, "ss_o")
                    DR(sc, "bar")

                @blockA.vector
                def _(v):
                    W(v, "s_prep", 1)
                    W(v, "s_bias", 64)
                    v.tensor_scalar(ident_bf[:, :], iota_f[:, :], pidx[:, :], None, OP.is_equal)
                    v.tensor_scalar(ident_f[:, :], iota_f[:, :], pidx[:, :], None, OP.is_equal)
                    v.tensor_scalar(sel_bf[:, :], sel_f[:, :], pidx[0:4, :], None, OP.is_equal)
                    DR(v, "s_sel")

                    def vstats(t):
                        W(v, "ss_acc", t + 1)
                        v.tensor_scalar(stt[:, 8:12], stt[:, 0:4], 1.0 / D, None, OP.mult)
                        v.tensor_scalar(stt[:, 12:16], stt[:, 4:8], 1.0 / D, None, OP.mult)
                        v.drain()
                        v.tensor_tensor(stt[:, 16:20], stt[:, 8:12], stt[:, 8:12], OP.mult)
                        v.drain()
                        v.tensor_tensor(stt[:, 20:24], stt[:, 12:16], stt[:, 16:20], OP.subtract)
                        v.drain()
                        v.tensor_scalar(stt[:, 24:28], stt[:, 20:24], 1.0, EPS1, OP.mult, OP.add)
                        v.tensor_scalar(stt[:, 28:32], stt[:, 20:24], 1.0 + EPS2, EPS1 * EPS2,
                                        OP.mult, OP.add)
                        v.drain()
                        v.reciprocal(stt[:, 32:40], stt[:, 24:32])
                        DR(v, "sv_st")

                    def xnts(t):
                        W(v, "ss_rr", t + 1)
                        W(v, "trA_R0", 4 * t)
                        W(v, "trA_R1", 4 * t)
                        W(v, "trB_R0", 4 * t)
                        W(v, "trB_R1", 4 * t)
                        for sub in range(SUB):
                            v.tensor_scalar(xn_tm[:, sub * D:(sub + 1) * D],
                                            x_buf[t % 2][:, sub * D:(sub + 1) * D],
                                            stt[:, 8 + sub:9 + sub], stt[:, 40 + sub:41 + sub],
                                            OP.subtract, OP.mult)
                        for sub in range(SUB):
                            v.tensor_scalar(xn2_tm[:, sub * D:(sub + 1) * D],
                                            x_buf[t % 2][:, sub * D:(sub + 1) * D],
                                            stt[:, 8 + sub:9 + sub], stt[:, 44 + sub:45 + sub],
                                            OP.subtract, OP.mult)
                        DR(v, "sv_xn", "s_xfree")

                    def trevacs(tau):
                        W(v, "s_xnw%d" % (tau % 2), 16 * (tau // 2))
                        W(v, "sv_s", tau - 1)
                        for k in range(KC):
                            g_ = 8 * tau + k
                            p_ = g_ % 2
                            W(v, "trA_R%d" % p_, g_ // 2 + 1)
                            ti(v.tensor_copy(xn_fm[tau % 2][:, k * TT:(k + 1) * TT],
                                             trA[p_][:, :]), "trA_F%d" % p_, 1)
                            W(v, "trB_R%d" % p_, g_ // 2 + 1)
                            ti(v.tensor_scalar(xn2_fm[:, k * TT:(k + 1) * TT], trB[p_][:, :],
                                               norw_cm[:, k:k + 1], None, OP.mult),
                               "trB_F%d" % p_, 1)
                        DR(v, "ss_xnev", "sv_x2ev")

                    vstats(0)
                    xnts(0)
                    vstats(1)
                    trevacs(0)
                    for t in range(NT):
                        for c in range(KC):
                            r = 24 * t + c
                            b = r % 2
                            W(v, "pj_R%d" % b, r // 2 + 1)
                            ti(v.tensor_copy(v_bf[:, c * TT:(c + 1) * TT], pj[b][:, :]),
                               "pj_F%d" % b, 1)
                        if t + 1 < NT:
                            xnts(t + 1)
                        for c in range(KC):
                            r = 24 * t + 8 + c
                            b = r % 2
                            W(v, "pj_F%d" % b, r // 2 + 1)
                            v.tensor_scalar(f_fm[:, c * TT:(c + 1) * TT],
                                            f_fm[:, c * TT:(c + 1) * TT], FORGET_LB, None,
                                            OP.max)
                        v.drain()
                        for h in range(KC):
                            b = h % 2
                            W(v, "vb_R%d" % b, 4 * t + h // 2 + 1)
                            ti(v.scalar_tensor_tensor(negu_bf[:, h * TT:(h + 1) * TT],
                                                      f_fm[:, h * TT:(h + 1) * TT], 1.0,
                                                      vbp[b][:, :], OP.subtract, OP.mult),
                               "vb_F%d" % b, 1)
                        v.drain()
                        if t % 4 != 0:
                            nS3 = negS_bf[:, :].rearrange("p (h tt) -> p h tt", h=KC)
                            v.tensor_copy(carry[:, :].rearrange("p (h o) -> p h o", o=1),
                                          nS3[:, :, TT - 1:TT])
                            v.drain()
                        for h in range(KC):
                            init = 0.0 if t % 4 == 0 else carry[:, h:h + 1]
                            v.tensor_tensor_scan(negS_bf[:, h * TT:(h + 1) * TT],
                                                 f_fm[:, h * TT:(h + 1) * TT],
                                                 negu_bf[:, h * TT:(h + 1) * TT],
                                                 init, OP.mult, OP.add)
                        if t + 2 < NT:
                            vstats(t + 2)
                        W(v, "s_lgR", t + 1)
                        lg3 = prt[:, 0:SUB * E].rearrange("p (s e) -> p s e", e=E)
                        mk3 = mask_all[:, t * SUB * E:(t + 1) * SUB * E].rearrange(
                            "p (s e) -> p s e", e=E)
                        st3 = stt[:, :].rearrange("p (c o) -> p c o", o=1)
                        for sub in range(SUB):
                            v.tensor_reduce(stt[:, 48 + sub:49 + sub], lg3[:, sub:sub + 1, :],
                                            mybir.AxisListType.X, OP.max)
                        v.drain()
                        rmax = st3[:, 48:52, :]
                        v.tensor_tensor(st3[:, 52:56, :], lg3[:, :, 0:1], rmax, OP.is_ge)
                        v.tensor_tensor(st3[:, 60:64, :], lg3[:, :, 1:2], rmax, OP.is_ge)
                        ins = v.tensor_tensor(st3[:, 64:68, :], lg3[:, :, 2:3], rmax, OP.is_ge)
                        ti(ins, "sv_lgF", 1)
                        v.drain()
                        v.tensor_scalar(st3[:, 56:60, :], st3[:, 52:56, :], -1.0, 1.0,
                                        OP.mult, OP.add)
                        v.tensor_copy(mk3[:, :, 0:1], st3[:, 52:56, :])
                        v.drain()
                        v.tensor_tensor(st3[:, 60:64, :], st3[:, 60:64, :], st3[:, 56:60, :],
                                        OP.mult)
                        v.drain()
                        v.tensor_tensor(st3[:, 56:60, :], st3[:, 56:60, :], st3[:, 60:64, :],
                                        OP.subtract)
                        v.tensor_copy(mk3[:, :, 1:2], st3[:, 60:64, :])
                        v.drain()
                        v.tensor_tensor(st3[:, 64:68, :], st3[:, 64:68, :], st3[:, 56:60, :],
                                        OP.mult)
                        v.drain()
                        v.tensor_copy(mk3[:, :, 2:3], st3[:, 64:68, :])
                        v.tensor_tensor(st3[:, 56:60, :], st3[:, 56:60, :], st3[:, 64:68, :],
                                        OP.subtract)
                        v.drain()
                        ti(v.tensor_copy(mk3[:, :, 3:4], st3[:, 56:60, :]), "sv_mtm", 1)
                        v.drain()
                        if t + 1 < NT:
                            trevacs(t + 1)
                        W(v, "ss_o", t + 1)
                        v.drain()
                        v.tensor_tensor(negu_bf[:, :], o_fm[:, :], negS_bf[:, :], OP.mult)
                        W(v, "s_sw", 16 * t)
                        W(v, "ss_xnev", t + 1)
                        v.drain()
                        ti(v.tensor_tensor(s_bf[:, :], xn_fm[t % 2][:, :], negu_bf[:, :],
                                           OP.subtract), "sv_s", 1)
                    DR(v, "bar")

                @blockA.tensor
                def _(te):
                    W(te, "s_sel", 1)
                    for t in range(NT):
                        W(te, "sv_xn", t + 1)
                        W(te, "sv_lgF", t)
                        for k in range(KC):
                            g_ = 8 * t + k
                            p_ = g_ % 2
                            W(te, "trA_F%d" % p_, g_ // 2)
                            ins = None
                            for sub in range(SUB):
                                ins = te.transpose(
                                    trA[p_][:, sub * 128:(sub + 1) * 128],
                                    xn_tm[:, sub * D + k * 128: sub * D + (k + 1) * 128],
                                    ident_bf[:, :])
                            ti(ins, "trA_R%d" % p_, 1)
                            W(te, "trB_F%d" % p_, g_ // 2)
                            for sub in range(SUB):
                                ins = te.transpose(
                                    trB[p_][:, sub * 128:(sub + 1) * 128],
                                    xn2_tm[:, sub * D + k * 128: sub * D + (k + 1) * 128],
                                    ident_bf[:, :])
                            ti(ins, "trB_R%d" % p_, 1)
                        W(te, "ss_xnev", t + 1)
                        if t == 0:
                            W(te, "s_wr", 16)
                        ins = None
                        for sub in range(SUB):
                            for k in range(KC):
                                ins = te.matmul(
                                    prt[:, sub * E:(sub + 1) * E],
                                    xn_fm[t % 2][:, k * TT + sub * 128: k * TT + (sub + 1) * 128],
                                    wr_bf[:, k * E:(k + 1) * E],
                                    start=(k == 0), stop=(k == KC - 1))
                        ti(ins, "s_lgR", 1)
                        W(te, "sv_x2ev", t + 1)
                        for pi, wT in enumerate([wiT, wfT, woT]):
                            if t == 0:
                                if pi == 0:
                                    W(te, "s_wt", 16 * 8)
                                elif pi == 1:
                                    W(te, "s_wtB", 16 * 8)
                                else:
                                    W(te, "s_wt2", 16 * 8)
                            for c in range(KC):
                                r = 24 * t + pi * KC + c
                                b = r % 2
                                W(te, "pj_F%d" % b, r // 2)
                                ins = None
                                for k in range(KC):
                                    ins = te.matmul(pj[b][:, :],
                                                    wT[:, k * D + c * 128: k * D + (c + 1) * 128],
                                                    xn2_fm[:, k * TT:(k + 1) * TT],
                                                    start=(k == 0), stop=(k == KC - 1))
                                ti(ins, "pj_R%d" % b, 1)
                            if pi == 1:
                                for h in range(KC):
                                    b = h % 2
                                    r = 24 * t + h
                                    W(te, "pj_F%d" % (r % 2), r // 2 + 1)
                                    W(te, "vb_F%d" % b, 4 * t + h // 2)
                                    ti(te.matmul(vbp[b][:, :], ones_bf[:, :],
                                                 v_bf[:, h * TT:(h + 1) * TT],
                                                 start=True, stop=True), "vb_R%d" % b, 1)
                    DR(te, "bar")

        with contextlib.ExitStack() as esB:
            EB = esB.enter_context
            w1sl = [EB(nc.sbuf_tensor("w1sl0", [128, KC * E * G * 128], BF16)),
                    EB(nc.sbuf_tensor("w1sl1", [128, KC * E * G * 128], BF16))]
            wosl = [EB(nc.sbuf_tensor("wosl0", [128, IC * 128], BF16)),
                    EB(nc.sbuf_tensor("wosl1", [128, IC * 128], BF16))]
            maskedA = EB(nc.sbuf_tensor("maskedA", [128, E * KC * TT], BF16))
            interA = EB(nc.sbuf_tensor("interA", [128, IC * TT], BF16))
            xn_b = [EB(nc.sbuf_tensor("xn_b0", [128, KC * TT], BF16)),
                    EB(nc.sbuf_tensor("xn_b1", [128, KC * TT], BF16))]
            s_b = EB(nc.sbuf_tensor("s_b", [128, KC * TT], BF16))
            y3 = EB(nc.sbuf_tensor("y3", [128, KC * TT], F32))
            g_bf = EB(nc.sbuf_tensor("g_bf", [128, KC * TT], BF16))
            mask4_bf = EB(nc.sbuf_tensor("mask4_bf", [4, TT], BF16))
            mu_rep = EB(nc.sbuf_tensor("mu_rep", [128, TT], F32))
            rstd_rep = EB(nc.sbuf_tensor("rstd_rep", [128, TT], F32))
            mb_bf = EB(nc.sbuf_tensor("mb_bf", [128, E * TT], BF16))
            jrow = EB(nc.sbuf_tensor("jrow", [128, TT], F32))
            p_bf = EB(nc.sbuf_tensor("p_bf", [128, KC * TT], BF16))


            out_tm = y3

            pre = [EB(nc.psum_tensor("pre0", [128, TT], F32)),
                   EB(nc.psum_tensor("pre1", [128, TT], F32))]
            y2 = [EB(nc.psum_tensor("y2a", [128, TT], F32)),
                  EB(nc.psum_tensor("y2b", [128, TT], F32))]
            pmu = EB(nc.psum_tensor("pmu", [128, TT], F32))
            psq = EB(nc.psum_tensor("psq", [128, TT], F32))
            pmb = EB(nc.psum_tensor("pmb", [128, TT], F32))
            pmb2 = EB(nc.psum_tensor("pmb2", [128, TT], F32))
            pmb_bf = pmb.bitcast(BF16)
            pmb2_bf = pmb2.bitcast(BF16)
            pmu_bf = pmu.bitcast(BF16)
            psq_bf = psq.bitcast(BF16)
            pre4 = [pre[0], pre[1], pmb, pmb2]

            y3bf = p_bf
            y3sq = g_bf

            def gsl(k0, k1):
                return g_bf[:, k0 * TT:k1 * TT]

            with nc.Block() as blockB:

                @blockB.sync
                def _(sy):
                    W(sy, "bar", 5)
                    ti(sy.dma_start(out=xn_b[0][:, :], in_=xnbf_s[0:128, :]), "s_xb0", 16)
                    for t in range(NT):
                        W(sy, "s_xfree", t + 7)
                        src = xin[t * TT:(t + 1) * TT, :].rearrange("(s p) d -> p s d", p=128)
                        dst = x_buf[t % 2][:, :].rearrange("p (s d) -> p s d", s=SUB)
                        ti(sy.dma_start(out=dst, in_=src), "s_x%d" % (t % 2), 16)
                        if t + 1 < NT:
                            W(sy, "y2_R1", 4 * t)
                            W(sy, "sv_msk", t)
                            ti(sy.dma_start(out=xn_b[(t + 1) % 2][:, :],
                                            in_=xnbf_s[(t + 1) * 128:(t + 2) * 128, :]),
                               "s_xb%d" % ((t + 1) % 2), 16)
                        for grp in range(NG):
                            gi = t * NG + grp
                            if gi == 0:
                                W(sy, "gp_cst", 4240)
                            W(sy, "sl_F", gi - 1)
                            ti(sy.dma_start(out=w1sl[gi % 2][:, :],
                                            in_=w1s[:, grp * 4096:(grp + 1) * 4096]),
                               "sl_R%d" % (gi % 2), 16)
                        for h in range(KC):
                            wi_ = t * KC + h
                            W(sy, "wo_F", wi_ - 1)
                            ti(sy.dma_start(out=wosl[wi_ % 2][:, :],
                                            in_=wos[:, h * I_FF:(h + 1) * I_FF]),
                               "wo_R%d" % (wi_ % 2), 16)
                        W(sy, "sv_p", t)
                        ti(sy.dma_start(out=s_b[:, :], in_=ss_s[t * 128:(t + 1) * 128, :]),
                           "s_sb", 16)
                    sy.wait_ge(SM["st_out"], 16 * NT)

                @blockB.gpsimd
                def _(g):
                    W(g, "bar", 5)
                    for t in range(NT):
                        W(g, "sv_out", t + 1)
                        W(g, "st_out", 16 * t)
                        dsto = out[t * TT:(t + 1) * TT, :].rearrange("(s p) d -> p s d", p=128)
                        srco = out_tm[:, :].rearrange("p (s d) -> p s d", s=SUB)
                        ti(g.dma_start(out=dsto, in_=srco), "st_out", 16)
                    g.wait_ge(SM["st_out"], 16 * NT)

                @blockB.tensor
                def _(te):
                    W(te, "bar", 5)

                    def emit_ptr(tau):
                        W(te, "sv_p", tau + 1)
                        for sub in range(SUB):
                            W(te, "mu_F", 5 * tau + sub + 1)
                            ins = None
                            for kk in range(4):
                                ins = te.transpose(
                                    pmu_bf[:, kk * 128:(kk + 1) * 128],
                                    p_bf[:, kk * TT + sub * 128: kk * TT + (sub + 1) * 128],
                                    ident_bf[:, :])
                            ti(ins, "mu_R", 1)
                            W(te, "sq_F", 5 * tau + sub + 1)
                            for kk in range(4):
                                ins = te.transpose(
                                    psq_bf[:, kk * 128:(kk + 1) * 128],
                                    p_bf[:, (4 + kk) * TT + sub * 128: (4 + kk) * TT + (sub + 1) * 128],
                                    ident_bf[:, :])
                            ti(ins, "sq_R", 1)

                    def mask_prep(tau):
                        W(te, "ss_mb", 2 * tau)
                        if tau > 0:
                            W(te, "pre_F2", 8 * tau)
                            W(te, "pre_F3", 8 * tau)
                        ins = None
                        for sub in range(SUB):
                            ins = te.transpose(
                                pmb2_bf[0:4, sub * 128:(sub + 1) * 128],
                                mask_all[:, (tau * SUB + sub) * E:(tau * SUB + sub + 1) * E],
                                ident_bf[:, :])
                        ti(ins, "s_mtR", 1)
                        W(te, "sv_m4", tau + 1)
                        te.matmul(pmb[:, :], sel_bf[:, 0:128], mask4_bf[:, :],
                                  start=True, stop=True)
                        ti(te.matmul(pmb2[:, :], sel_bf[:, 128:256], mask4_bf[:, :],
                                     start=True, stop=True), "s_mbR", 1)
                        W(te, "ss_mb", 2 * tau + 1)
                        te.matmul(pmb[:, :], sel_bf[:, 256:384], mask4_bf[:, :],
                                  start=True, stop=True)
                        ti(te.matmul(pmb2[:, :], sel_bf[:, 384:512], mask4_bf[:, :],
                                     start=True, stop=True), "s_mbR", 1)

                    mask_prep(0)
                    for t in range(NT):
                        W(te, "sv_msk", t + 1)
                        for grp in range(NG):
                            if grp == NG // 2 and t > 0:
                                emit_ptr(t - 1)
                            gi = t * NG + grp
                            W(te, "sl_R%d" % (gi % 2), 16 * (gi // 2 + 1))
                            for il in range(G):
                                i_abs = grp * G + il
                                b = i_abs % 4
                                pr4 = t * (IC // 4) + i_abs // 4
                                W(te, "pre_F%d" % b, pr4)
                                if b >= 2 and i_abs < 4:
                                    W(te, "ss_mb", 2 * (t + 1))
                                first = True
                                for e in range(E):
                                    for k in range(KC):
                                        off = ((k * E + e) * G + il) * 128
                                        ins = te.matmul(
                                            pre4[b][:, :], w1sl[gi % 2][:, off:off + 128],
                                            maskedA[:, (e * KC + k) * TT:(e * KC + k + 1) * TT],
                                            start=first, stop=False)
                                        first = False
                                        if il == G - 1 and e == E - 1 and k == KC - 1:
                                            ti(ins, "sl_F", 1)
                                ti(te.matmul(pre4[b][:, :],
                                             b1t_bf[:, i_abs * 128:(i_abs + 1) * 128],
                                             mask4_bf[:, :], start=False, stop=True),
                                   "pre_R%d" % b, 1)
                        if t + 1 < NT:
                            mask_prep(t + 1)
                        W(te, "ss_int", t + 1)
                        for h in range(KC):
                            wi_ = t * KC + h
                            W(te, "wo_R%d" % (wi_ % 2), 16 * (wi_ // 2 + 1))
                            b = h % 2
                            W(te, "y2_F%d" % b, wi_ // 2)
                            ins = None
                            for ic in range(IC):
                                ins = te.matmul(y2[b][:, :],
                                                wosl[wi_ % 2][:, ic * 128:(ic + 1) * 128],
                                                interA[:, ic * TT:(ic + 1) * TT],
                                                start=(ic == 0), stop=False)
                            ti(ins, "wo_F", 1)
                            ti(te.matmul(y2[b][:, :], ident_bf[:, :],
                                         xn_b[t % 2][:, h * TT:(h + 1) * TT],
                                         start=False, stop=True), "y2_R%d" % b, 1)
                        W(te, "sv_y3bf", t + 1)
                        W(te, "mu_F", 5 * t)
                        W(te, "sq_F", 5 * t)
                        ins = None
                        for k in range(KC):
                            ins = te.matmul(pmu[:, :], ones_bf[:, :],
                                            y3bf[:, k * TT:(k + 1) * TT],
                                            start=(k == 0), stop=(k == KC - 1))
                        ti(ins, "mu_R", 1)
                        for k in range(KC):
                            ins = te.matmul(psq[:, :], ones_bf[:, :],
                                            y3sq[:, k * TT:(k + 1) * TT],
                                            start=(k == 0), stop=(k == KC - 1))
                        ti(ins, "sq_R", 1)
                    emit_ptr(NT - 1)

                @blockB.vector
                def _(v):
                    W(v, "bar", 5)

                    def final_adds(tau):
                        W(v, "s_x%d" % (tau % 2), 16 * (NT // 2 + tau // 2 + 1))
                        for sub in range(SUB):
                            W(v, "mu_R", 5 * tau + 2 + sub)
                            ti(v.tensor_tensor(out_tm[:, sub * D: sub * D + TT],
                                               x_buf[tau % 2][:, sub * D: sub * D + TT],
                                               pmu_bf[:, 0:TT], OP.add), "mu_F", 1)
                            W(v, "sq_R", 5 * tau + 2 + sub)
                            ti(v.tensor_tensor(out_tm[:, sub * D + TT: (sub + 1) * D],
                                               x_buf[tau % 2][:, sub * D + TT: (sub + 1) * D],
                                               psq_bf[:, 0:TT], OP.add), "sq_F", 1)
                        DR(v, "sv_out", "s_xfree")

                    def tail(tau):
                        W(v, "mu_R", 5 * tau + 1)
                        W(v, "sq_R", 5 * tau + 1)
                        ti(v.tensor_scalar(mu_rep[:, :], pmu[:, :], 1.0 / D, None, OP.mult),
                           "mu_F", 1)
                        ti(v.tensor_scalar(rstd_rep[:, :], psq[:, :], 1.0 / D, None, OP.mult),
                           "sq_F", 1)
                        v.drain()
                        v.tensor_tensor(jrow[:, :], mu_rep[:, :], mu_rep[:, :], OP.mult)
                        v.drain()
                        v.scalar_tensor_tensor(rstd_rep[:, :], rstd_rep[:, :], EPS3,
                                               jrow[:, :], OP.add, OP.subtract)
                        v.drain()
                        v.reciprocal(jrow[:, :], rstd_rep[:, :])
                        DR(v, "sv_var")
                        W(v, "ss_sqrt", tau + 1)
                        for k in range(KC):
                            v.tensor_tensor(y3[:, k * TT:(k + 1) * TT],
                                            y3[:, k * TT:(k + 1) * TT], mu_rep[:, :],
                                            OP.subtract)
                        v.drain()
                        for k in range(KC):
                            v.tensor_tensor(y3[:, k * TT:(k + 1) * TT],
                                            y3[:, k * TT:(k + 1) * TT], rstd_rep[:, :],
                                            OP.mult)
                        DR(v, "sv_z")
                        W(v, "ss_g", 8 * (tau + 1))
                        W(v, "s_sb", 16 * (tau + 1))
                        ti(v.tensor_tensor(p_bf[:, :], s_b[:, :], gsl(0, KC), OP.mult),
                           "sv_p", 1)

                    def mask_block(tau):
                        W(v, "s_mtR", tau + 1)
                        W(v, "pre_R3", 8 * tau)
                        ti(v.tensor_copy(mask4_bf[:, :], pmb2_bf[0:4, 0:TT]), "sv_m4", 1)
                        W(v, "ss_mb", 2 * (tau + 1))
                        W(v, "s_xb%d" % (tau % 2), 16 * (tau // 2 + 1))
                        ins = None
                        for e in range(E):
                            for k in range(KC):
                                ins = v.tensor_tensor(
                                    maskedA[:, (e * KC + k) * TT:(e * KC + k + 1) * TT],
                                    xn_b[tau % 2][:, k * TT:(k + 1) * TT],
                                    mb_bf[:, e * TT:(e + 1) * TT], OP.mult)
                        DR(v, "sv_msk")

                    mask_block(0)
                    for t in range(NT):
                        if t > 0:
                            tail(t - 1)
                            final_adds(t - 1)
                        if t + 1 < NT:
                            mask_block(t + 1)
                        W(v, "mu_R", 5 * t)
                        W(v, "sq_R", 5 * t)
                        for h in range(KC):
                            W(v, "y2_F%d" % (h % 2), (t * KC + h) // 2 + 1)
                            v.tensor_copy(y3bf[:, h * TT:(h + 1) * TT],
                                          y3[:, h * TT:(h + 1) * TT])
                            v.tensor_tensor(y3sq[:, h * TT:(h + 1) * TT],
                                            y3[:, h * TT:(h + 1) * TT],
                                            y3[:, h * TT:(h + 1) * TT], OP.mult)
                        DR(v, "sv_y3bf")
                    tail(NT - 1)
                    final_adds(NT - 1)

                @blockB.scalar
                def _(sc):
                    W(sc, "bar", 5)

                    def sqrtp(tau):
                        W(sc, "sv_var", tau + 1)
                        ti(sc.activation(rstd_rep[:, :], jrow[:, :], AF.Sqrt), "ss_sqrt", 1)

                    def silup(tau):
                        W(sc, "sv_z", tau + 1)
                        for k in range(KC):
                            ti(sc.activation(gsl(k, k + 1), y3[:, k * TT:(k + 1) * TT],
                                             AF.Silu, bias=bet_cm[:, k:k + 1],
                                             scale=gam_cm[:, k:k + 1]), "ss_g", 1)

                    def mb_evacs(tau):
                        W(sc, "s_mbR", 2 * tau + 1)
                        sc.copy(mb_bf[:, 0:TT], pmb[:, :])
                        ti(sc.copy(mb_bf[:, TT:2 * TT], pmb2[:, :]), "ss_mb", 1)
                        W(sc, "s_mbR", 2 * tau + 2)
                        sc.copy(mb_bf[:, 2 * TT:3 * TT], pmb[:, :])
                        ti(sc.copy(mb_bf[:, 3 * TT:4 * TT], pmb2[:, :]), "ss_mb", 1)

                    mb_evacs(0)
                    for t in range(NT):
                        sc.drain()
                        W(sc, "y2_R1", 4 * t)
                        for i_abs in range(IC):
                            if i_abs == 4 and t > 0:
                                sqrtp(t - 1)
                            if i_abs == 12 and t > 0:
                                silup(t - 1)
                            b = i_abs % 4
                            pr4 = t * (IC // 4) + i_abs // 4
                            W(sc, "pre_R%d" % b, pr4 + 1)
                            ti(sc.activation(interA[:, i_abs * TT:(i_abs + 1) * TT],
                                             pre4[b][:, :], AF.Gelu), "pre_F%d" % b, 1)
                        DR(sc, "ss_int")
                        if t + 1 < NT:
                            mb_evacs(t + 1)
                        W(sc, "st_out", 16 * t)
                        for h in range(KC):
                            wi_ = t * KC + h
                            b = h % 2
                            W(sc, "y2_R%d" % b, wi_ // 2 + 1)
                            ti(sc.activation(y3[:, h * TT:(h + 1) * TT], y2[b][:, :],
                                             AF.Identity, bias=bo_cm[:, h:h + 1]),
                               "y2_F%d" % b, 1)
                        DR(sc, "ss_y3")
                    sqrtp(NT - 1)
                    silup(NT - 1)


    return nc


_NC_CACHE = {}


def kernel(L0_attn_x, L0_attn_norm_w, L0_attn_w_forget, L0_attn_w_input,
           L0_attn_w_output, L0_ffn_W_router, L0_ffn_W1, L0_ffn_b1, L0_ffn_Wo,
           L0_ffn_bo, L0_ffn_ln_gamma, L0_ffn_ln_beta):
    x = np.ascontiguousarray(np.asarray(L0_attn_x, dtype=np.float32))
    weights = {
        "wf": np.ascontiguousarray(np.asarray(L0_attn_w_forget, np.float32)),
        "wiw": np.ascontiguousarray(np.asarray(L0_attn_w_input, np.float32)),
        "wow": np.ascontiguousarray(np.asarray(L0_attn_w_output, np.float32)),
        "norw": np.ascontiguousarray(np.asarray(L0_attn_norm_w, np.float32)),
        "wr": np.ascontiguousarray(np.asarray(L0_ffn_W_router, np.float32)),
        "w1": np.ascontiguousarray(np.asarray(L0_ffn_W1, np.float32)),
        "b1": np.ascontiguousarray(np.asarray(L0_ffn_b1, np.float32)),
        "wo": np.ascontiguousarray(np.asarray(L0_ffn_Wo, np.float32)),
        "bo": np.ascontiguousarray(np.asarray(L0_ffn_bo, np.float32)),
        "gam": np.ascontiguousarray(np.asarray(L0_ffn_ln_gamma, np.float32)),
        "bet": np.ascontiguousarray(np.asarray(L0_ffn_ln_beta, np.float32)),
    }
    if "nc" not in _NC_CACHE:
        _NC_CACHE["nc"] = build_core()
    nc = _NC_CACHE["nc"]
    in_maps = []
    for c in range(8):
        shard = x[2 * c:2 * c + 2].reshape(TCORE, D)
        m = {"x": np.ascontiguousarray(shard)}
        m.update(weights)
        in_maps.append(m)
    res = run_bass_kernel_spmd(nc, in_maps, core_ids=list(range(8)))
    outs = [np.asarray(r["out"]).reshape(2, 2048, D) for r in res.results]
    return np.concatenate(outs, axis=0).astype(np.float32)
